# version 20
# baseline (speedup 1.0000x reference)
import numpy as np
import ml_dtypes
import concourse.bass as bass
import concourse.mybir as mybir
from concourse.bass_utils import run_bass_kernel_spmd

F32 = mybir.dt.float32
BF16 = mybir.dt.bfloat16
I32 = mybir.dt.int32
Alu = mybir.AluOpType
Act = mybir.ActivationFunctionType
AX = mybir.AxisListType

D = 1024
E = 2048
HID = 4096
KC = 8
ECN = 16
HCN = 32
TG = 512
HALF = 256
NSUB_G = 4
CORE_T = 4096
PADL = 128
HALO = 128
IN_T = PADL + CORE_T + HALO
BUFW = 16 + TG + 16
AWIN = 528
FWIN = 260

ENGS = ("pe", "dve", "act", "gp", "sq")
DMAQ = {"qs": "sq", "qa": "act", "qg": "gp"}
NSEMQ = 6


class Em:

    def __init__(self, nc, self_waits=False):
        self.nc = nc
        self.self_waits = self_waits
        self.ops = {e: [] for e in ENGS}
        self.cnt = {e: 0 for e in ENGS}
        self.dcnt = {f"{q}{i}": 0 for q in DMAQ for i in range(NSEMQ)}
        self.didx = {q: 0 for q in DMAQ}
        self.waited = {s: {} for s in ENGS}
        self.reg = {}
        self.sem_handles = {}

    def _r(self, key):
        if key not in self.reg:
            self.reg[key] = {"w": None, "r": {}}
        return self.reg[key]

    def _edges(self, reads, writes):
        need = {}
        def add(tok):
            if tok is None:
                return
            s, v = tok
            if need.get(s, 0) < v:
                need[s] = v
        for k in reads:
            add(self._r(k)["w"])
        for k in writes:
            r = self._r(k)
            add(r["w"])
            for s, v in r["r"].items():
                add((s, v))
        return need

    def _emit_waits(self, stream, need, self_sem):
        for s, v in sorted(need.items()):
            if s == self_sem and s == "pe" and not self.self_waits:
                continue
            if self.waited[stream].get(s, 0) >= v:
                continue
            self.waited[stream][s] = v
            self.ops[stream].append(("w", s, v))

    def _update(self, reads, writes, tok):
        for k in reads:
            r = self._r(k)
            s, v = tok
            if r["r"].get(s, 0) < v:
                r["r"][s] = v
        for k in writes:
            r = self._r(k)
            r["w"] = tok
            r["r"] = {}

    def op(self, eng, fn, reads=(), writes=(), tr=False):
        need = self._edges(reads, writes)
        if eng == "pe" and tr != getattr(self, "pe_mode", False):
            self.pe_mode = tr
            if self.cnt["pe"] > 0 and not self.self_waits:
                self.ops["pe"].append(("w", "pe", self.cnt["pe"]))
                self.waited["pe"]["pe"] = self.cnt["pe"]
        self._emit_waits(eng, need, eng)
        self.cnt[eng] += 1
        tok = (eng, self.cnt[eng])
        self.ops[eng].append(("i", fn, eng))
        self._update(reads, writes, tok)
        return tok

    def dma(self, q, fn, reads=(), writes=()):
        stream = DMAQ[q]
        need = self._edges(reads, writes)
        self._emit_waits(stream, need, None)
        sem = f"{q}{self.didx[q] % NSEMQ}"
        self.didx[q] += 1
        if self.dcnt[sem] > 0:
            self._emit_waits(stream, {sem: self.dcnt[sem]}, None)
        self.dcnt[sem] += 16
        tok = (sem, self.dcnt[sem])
        self.ops[stream].append(("d", fn, sem))
        self._update(reads, writes, tok)
        return tok

    def barrier_token(self, keys):
        need = {}
        for k in keys:
            r = self._r(k)
            if r["w"] is not None:
                s, v = r["w"]
                if need.get(s, 0) < v:
                    need[s] = v
        return need


def _mk_closure_runner(em):
    def run(stream, E):
        sems = em.sem_handles
        for item in em.ops[stream]:
            if item[0] == "w":
                _, s, v = item
                E.wait_ge(sems[s], v)
            elif item[0] == "i":
                _, fn, eng = item
                fn(E).then_inc(sems[eng], 1)
            else:
                _, fn, sem = item
                fn(E).then_inc(sems[sem], 16)
    return run


def build_nc(n_groups=8, n_sub=None, debug=False, sim_safe=False):
    NG = n_groups
    NSUB = 4 * NG + 1 if n_sub is None else n_sub
    OUT_T = TG * NG

    nc = bass.Bass("TRN2", target_bir_lowering=False,
                   detect_race_conditions=sim_safe)
    ap = lambda name, shape, dt, kind: nc.dram_tensor(name, shape, dt, kind=kind).ap()
    x_h = ap("xin", [IN_T, D], F32, "ExternalInput")
    W1_h = ap("W1", [D, E], F32, "ExternalInput")
    b1a_h = ap("b1a", [128, 8], F32, "ExternalInput")
    b1g_h = ap("b1g", [128, 8], F32, "ExternalInput")
    W2_h = ap("W2", [D, D], F32, "ExternalInput")
    b2b_h = ap("b2b", [128, D], F32, "ExternalInput")
    csw_h = ap("cswr", [128, 8 * 15], F32, "ExternalInput")
    w1T_h = ap("fw1t", [D, HID], F32, "ExternalInput")
    cw_h = ap("fcw", [128, HCN * 3], F32, "ExternalInput")
    cb_h = ap("fcb", [128, HCN], F32, "ExternalInput")
    w2T_h = ap("fw2t", [HID, D], F32, "ExternalInput")
    idb_h = ap("identb", [128, 128], BF16, "ExternalInput")
    out_h = ap("out", [OUT_T, D], F32, "ExternalOutput")
    scr_W1 = ap("scr_W1", [ECN, 128, KC * 128], BF16, "Internal")
    scr_q1 = ap("scr_q1", [HCN, 128, KC * 128], BF16, "Internal")
    scr_q2 = ap("scr_q2", [HCN, 128, D], BF16, "Internal")
    scr_W2 = ap("scr_W2", [KC, 128, D], BF16, "Internal")
    scr_dg = ap("scr_dg", [KC, 128, 15 * 128], BF16, "Internal")
    dbg_h = ap("dbg", [NSUB_G * 128, D], F32, "ExternalOutput") if debug else None
    dbg2_h = ap("dbg2", [128, HCN * HALF], F32, "ExternalOutput") if debug else None
    dbg3_h = ap("dbg3", [128, HALF + 16], F32, "ExternalOutput") if debug else None
    dbg4_h = ap("dbg4", [128, KC * BUFW], BF16, "ExternalOutput") if debug else None

    em = Em(nc, self_waits=sim_safe)
    from contextlib import ExitStack
    ctx = ExitStack()
    sb = lambda name, shape, dt=F32: ctx.enter_context(nc.sbuf_tensor(name, shape, dt))
    psalloc = lambda name, shape, dt: ctx.enter_context(nc.psum_tensor(name, shape, dt))

    idb = sb("idb", [128, 128], BF16)
    onesb = sb("onesb", [128, 128], BF16)
    slW2 = sb("slW2", [128, 2 * D], BF16)
    slW1 = sb("slW1", [128, 2 * KC * 128], BF16)
    slq1 = sb("slq1", [128, 2 * KC * 128], BF16)
    slq2 = sb("slq2", [128, 2 * D], BF16)
    xnT = sb("xnT", [128, 2 * KC * BUFW], BF16)
    nT = sb("nT", [128, 2 * KC * BUFW], BF16)
    xring = sb("xring", [128, 2 * D])
    xnring = sb("xnring", [128, 2 * D])
    xnbf = sb("xnbf", [128, 2 * D], BF16)
    nbf = sb("nbf", [128, 2 * D], BF16)
    pt1 = sb("pt1", [128, D])
    pt2 = sb("pt2", [128, D])
    pti = ctx.enter_context(nc.sbuf_tensor("pti", [128, D], I32))
    h2 = sb("h2", [128, HCN * HALF])
    cvA = sb("cvA", [128, 2 * HALF])
    cvB = sb("cvB", [128, 2 * HALF])
    cvY = sb("cvY", [128, 2 * HALF])
    Mhi = sb("Mhi", [128, HALF])
    Mlo = sb("Mlo", [128, HALF])
    sB = sb("sB", [128, HALF])
    shi = sb("shi", [128, HALF], BF16)
    smid = sb("smid", [128, HALF], BF16)
    slo = sb("slo", [128, HALF], BF16)
    st1 = sb("st1", [128, HALF])
    st2 = sb("st2", [128, HALF])
    rc = sb("rc", [128, 128])
    scol = sb("scol", [128, 8])
    rsp = sb("rsp", [128, 2 * 3], BF16)
    rspt = sb("rspt", [128, 4])
    Srow = sb("Srow", [128, 3 * HALF], BF16)
    s2b = sb("s2b", [128, HALF])
    ht1 = sb("ht1", [128, 4 * HALF])
    hti = None
    htc = sb("htc", [128, 4 * HALF])
    htd = sb("htd", [128, 4 * HALF])
    n2T = sb("n2T", [128, 2 * 4 * HALF], BF16)
    sg = sb("sg", [128, 2 * AWIN])
    xglu = sb("xglu", [128, 8 * AWIN], BF16)
    dsc = sb("dsc", [128, 2 * 15 * 128], BF16)
    Mbig = sb("Mbig", [128, 1024])
    convT = sb("convT", [128, KC * TG], BF16)
    out_acc = sb("out_acc", [128, NSUB_G * D])
    xout = sb("xout", [128, D])
    b1a = sb("b1a_c", [128, 8])
    b1g = sb("b1g_c", [128, 8])
    b2b = sb("b2b_c", [128, D])
    cwc = sb("cw_c", [128, HCN * 3])
    cbc = sb("cb_c", [128, HCN])
    filt = sb("filt", [128, 8 * 15])
    fsc = sb("fsc", [128, 128])
    colk = sb("colk", [128, 16])
    qabs = sb("qabs", [128, 40])
    qldA = out_acc[:, 0:D]
    qldB = out_acc[:, D:2 * D]
    qv = out_acc[:, 2 * D:3 * D]
    qc = out_acc[:, 3 * D:4 * D]
    qw = convT[:, 0:D]
    qstg = convT[:, D:2 * D]

    pb = [psalloc(f"pb{i}", [128, 512], F32) for i in range(7)]
    pbT = psalloc("pbT", [128, 1024], BF16)

    for s in list(ENGS) + [f"{q}{i}" for q in DMAQ for i in range(NSEMQ)]:
        em.sem_handles[s] = ctx.enter_context(nc.semaphore(f"sem_{s}"))

    RD = lambda *k: tuple(k)

    def split3(x_ap, n, o_hi, o_mid, o_lo, s1_ap, s2_ap, rkey, wkeys, skeys, eng="dve"):
        hi_k, mid_k, lo_k = wkeys
        s1k, s2k = skeys
        em.op(eng, lambda E: E.tensor_copy(o_hi, x_ap), RD(rkey), RD(hi_k))
        em.op(eng, lambda E: E.tensor_copy(s1_ap, o_hi), RD(hi_k), RD(s1k))
        em.op(eng, lambda E: E.tensor_sub(s2_ap, x_ap, s1_ap), RD(rkey, s1k), RD(s2k))
        em.op(eng, lambda E: E.tensor_copy(o_mid, s2_ap), RD(s2k), RD(mid_k))
        em.op(eng, lambda E: E.tensor_copy(s1_ap, o_mid), RD(mid_k), RD(s1k))
        em.op(eng, lambda E: E.tensor_sub(s2_ap, s2_ap, s1_ap), RD(s2k, s1k), RD(s2k))
        em.op(eng, lambda E: E.tensor_copy(o_lo, s2_ap), RD(s2k), RD(lo_k))

    em.dma("qg", lambda E: E.dma_start(out=idb[:], in_=idb_h[:]), (), RD("idb"))
    em.dma("qg", lambda E: E.dma_start(out=b1a[:], in_=b1a_h[:]), (), RD("b1a"))
    em.dma("qg", lambda E: E.dma_start(out=b1g[:], in_=b1g_h[:]), (), RD("b1g"))
    em.dma("qg", lambda E: E.dma_start(out=b2b[:], in_=b2b_h[:]), (), RD("b2b"))
    em.dma("qg", lambda E: E.dma_start(out=cwc[:], in_=cw_h[:]), (), RD("cwc"))
    em.dma("qg", lambda E: E.dma_start(out=cbc[:], in_=cb_h[:]), (), RD("cbc"))
    em.dma("qg", lambda E: E.dma_start(out=fsc[:, 0:8 * 15].rearrange("p (a b) -> p a b", b=15),
                                       in_=csw_h[:].rearrange("p (a b) -> p a b", b=15)), (), RD("fsc0"))
    em.op("gp", lambda E: E.memset(onesb[:], 1.0), (), RD("onesb"))
    em.op("gp", lambda E: E.memset(Srow[:], 0.0), (), RD("Srow_z"))
    em.op("gp", lambda E: E.memset(colk[:, 0:1], 0.5), (), RD("halfcol"))
    em.op("dve", lambda E: E.memset(rsp[:], 0.0), (), RD("rsp"))
    em.op("dve", lambda E: E.memset(rspt[:], 0.0), (), RD("rsp"))

    csw3 = fsc[:, 0:120].rearrange("p (a b) -> p a b", b=15)
    fmx = fsc[:, 120:128].rearrange("p (a b) -> p a b", b=1)
    em.op("dve", lambda E: E.reduce_max(fmx, csw3, AX.X), RD("fsc0"), RD("fmx"))
    filt3 = filt[:].rearrange("p (a b) -> p a b", b=15)
    em.op("dve", lambda E: E.tensor_sub(filt3, csw3, fmx.to_broadcast([128, 8, 15])),
          RD("fsc0", "fmx"), RD("filt"))
    em.op("act", lambda E: E.activation(filt[:], filt[:], Act.Exp), RD("filt"), RD("filt"))
    em.op("dve", lambda E: E.reduce_sum(fmx, filt3, AX.X), RD("filt"), RD("fmx"))
    em.op("dve", lambda E: E.reciprocal(fsc[:, 120:128], fsc[:, 120:128]), RD("fmx"), RD("fmx"))
    em.op("dve", lambda E: E.tensor_mul(filt3, filt3, fmx.to_broadcast([128, 8, 15])),
          RD("filt", "fmx"), RD("filt"))
    for j in range(KC):
        for k in range(15):
            em.op("dve", lambda E, j=j, k=k: E.tensor_scalar_mul(
                convT[:, k * 128:(k + 1) * 128], idb[:], filt[:, j * 15 + k:j * 15 + k + 1]),
                RD("idb", "filt"), RD("convT0", "convT1", "convT2", "convT3"))
        em.dma("qs", lambda E, j=j: E.dma_start(out=scr_dg[j], in_=convT[:, 0:15 * 128]),
               RD("convT0", "convT1", "convT2", "convT3"), RD(f"scrdg_{j}"))

    for dc in range(KC):
        em.dma("qg", lambda E, dc=dc: E.dma_start(out=qldA[:], in_=W2_h[dc * 128:(dc + 1) * 128, :]),
               (), RD("oacc0"))
        em.op("gp", lambda E, dc=dc: E.tensor_copy(qw[:, 0:D], qldA[:]),
              RD("oacc0"), RD("convT0", "convT1"))
        em.dma("qg", lambda E, dc=dc: E.dma_start(out=scr_W2[dc], in_=qw[:, 0:D]),
               RD("convT0", "convT1"), RD(f"scrW2_{dc}"))

    def colsum(cols, outcol, tag):
        em.op("dve", lambda E, a=cols[0], b=cols[1], o=outcol:
              E.tensor_add(qabs[:, o:o + 1], qabs[:, a:a + 1], qabs[:, b:b + 1]),
              RD(f"qab{cols[0]}", f"qab{cols[1]}"), RD(f"qab{outcol}"))
        for c in cols[2:]:
            em.op("dve", lambda E, c=c, o=outcol:
                  E.tensor_add(qabs[:, o:o + 1], qabs[:, o:o + 1], qabs[:, c:c + 1]),
                  RD(f"qab{c}", f"qab{outcol}"), RD(f"qab{outcol}"))

    def psum_partition_sum(colidx, outcol, scale_mul, tag):
        ck = f"qab{colidx}"
        split3(qabs[:, colidx:colidx + 1], 1,
               rsp[:, 0:1], rsp[:, 1:2], rsp[:, 2:3], rspt[:, 0:1], rspt[:, 1:2],
               ck, ("rsp", "rsp", "rsp"), ("rsp", "rsp"))
        for k in range(3):
            em.op("pe", lambda E, k=k: E.matmul(pb[3][0:1, 0:1], rsp[:, k:k + 1], onesb[:, 0:1],
                                                start=(k == 0), stop=(k == 2)),
                  RD("rsp", "onesb"), RD("pb3"))
        em.op("dve", lambda E: E.tensor_scalar(st1[0:1, 0:1], pb[3][0:1, 0:1], scale_mul, 1e-5,
                                               Alu.mult, Alu.max),
              RD("pb3"), RD(f"sc{tag}"))
        split3(st1[0:1, 0:1], 1, rsp[0:1, 3:4], rsp[0:1, 4:5], rsp[0:1, 5:6],
               rspt[0:1, 2:3], rspt[0:1, 3:4],
               f"sc{tag}", ("rsp", "rsp", "rsp"), ("rsp", "rsp"))
        for k in range(3):
            em.op("pe", lambda E, k=k: E.matmul(pb[3][:, 8:9], onesb[:], rsp[:, 3 + k:4 + k],
                                                start=(k == 0), stop=(k == 2)),
                  RD("rsp", "onesb"), RD("pb3"))
        em.op("dve", lambda E: E.tensor_copy(qabs[:, outcol:outcol + 1], pb[3][:, 8:9]),
              RD("pb3"), RD(f"qab{outcol}"))

    def quant_chunk(buf_ap, bk, scale_col, scale_key, out_bf_ap, out_keys, width):
        em.op("dve", lambda E: E.tensor_scalar_mul(qv[:, 0:width], buf_ap, scale_col),
              RD(bk, scale_key), RD("oacc2"))
        em.op("dve", lambda E: E.tensor_single_scalar(qc[:, 0:width], qv[:, 0:width], 0.5, Alu.is_ge),
              RD("oacc2"), RD("oacc3"))
        em.op("dve", lambda E: E.tensor_single_scalar(qv[:, 0:width], qv[:, 0:width], -0.5, Alu.is_le),
              RD("oacc2"), RD("oacc2"))
        em.op("dve", lambda E: E.tensor_sub(out_bf_ap, qc[:, 0:width], qv[:, 0:width]),
              RD("oacc3", "oacc2"), out_keys)

    startup_units = []

    def _mu_w1(i):
        buf = qldA if i % 2 == 0 else qldB
        bk = "oacc0" if i % 2 == 0 else "oacc1"
        kc, qq = i // 4, i % 4
        em.dma(("qg", "qs", "qa")[i % 3], lambda E, kc=kc, qq=qq, buf=buf: E.dma_start(
            out=buf[:], in_=w1T_h[kc * 128:(kc + 1) * 128, qq * D:(qq + 1) * D]), (), RD(bk))
        em.op("act", lambda E, buf=buf, i=i: E.activation(qc[:], buf[:], Act.Abs,
                                                          accum_out=qabs[:, i % 32:i % 32 + 1]),
              RD(bk), RD("oacc3", f"qab{i % 32}"))

    def _mu_w2(i):
        buf = qldA if i % 2 == 0 else qldB
        bk = "oacc0" if i % 2 == 0 else "oacc1"
        em.dma(("qa", "qg", "qs")[i % 3], lambda E, i=i, buf=buf: E.dma_start(
            out=buf[:], in_=w2T_h[i * 128:(i + 1) * 128, :]), (), RD(bk))
        em.op("act", lambda E, buf=buf: E.activation(qc[:], buf[:], Act.Abs,
                                                     accum_out=qabs[:, 36:37]),
              RD(bk), RD("oacc3", "qab36"))
        if i == 0:
            em.op("dve", lambda E: E.tensor_copy(qabs[:, 38:39], qabs[:, 36:37]),
                  RD("qab36"), RD("qab38"))
        else:
            em.op("dve", lambda E: E.tensor_add(qabs[:, 38:39], qabs[:, 38:39], qabs[:, 36:37]),
                  RD("qab36", "qab38"), RD("qab38"))

    def _scales_w1():
        colsum(list(range(32)), 32, "w1")
        psum_partition_sum(32, 33, 1.0 / (HID * D), "m1")
        em.op("dve", lambda E: E.reciprocal(qabs[:, 34:35], qabs[:, 33:34]), RD("qab33"), RD("qab34"))
        em.op("dve", lambda E: E.tensor_scalar_mul(cwc[:], cwc[:], qabs[:, 33:34]),
              RD("cwc", "qab33"), RD("cwc"))

    def _scales_w2():
        psum_partition_sum(38, 35, 1.0 / (D * HID), "m2")
        em.op("dve", lambda E: E.reciprocal(qabs[:, 39:40], qabs[:, 35:36]), RD("qab35"), RD("qab39"))

    def _qu_w1(hc):
        buf = qldA if hc % 2 == 0 else qldB
        bk = "oacc0" if hc % 2 == 0 else "oacc1"
        w1v = w1T_h.rearrange("(kc p) h -> kc p h", p=128)
        q = ("qs", "qa", "qg")[hc % 3]
        em.dma(q, lambda E, hc=hc, buf=buf: E.dma_start(
            out=buf[:].rearrange("p (kc c) -> p kc c", c=128),
            in_=w1v[:, :, hc * 128:(hc + 1) * 128].rearrange("kc p c -> p kc c")),
            (), RD(bk))
        quant_chunk(buf[:], bk, qabs[:, 34:35], "qab34", qw[:, 0:D], ("convT0", "convT1"), D)
        em.dma(q, lambda E, hc=hc: E.dma_start(out=scr_q1[hc], in_=qw[:, 0:D]),
               RD("convT0", "convT1"), RD(f"scrq1_{hc}"))

    def _qu_w2(hc):
        buf = qldA if hc % 2 == 0 else qldB
        bk = "oacc0" if hc % 2 == 0 else "oacc1"
        q = ("qa", "qg", "qs")[hc % 3]
        em.dma(q, lambda E, hc=hc, buf=buf: E.dma_start(
            out=buf[:], in_=w2T_h[hc * 128:(hc + 1) * 128, :]), (), RD(bk))
        quant_chunk(buf[:], bk, qabs[:, 39:40], "qab39", qw[:, 0:D], ("convT0", "convT1"), D)
        em.dma(q, lambda E, hc=hc: E.dma_start(out=scr_q2[hc], in_=qw[:, 0:D]),
               RD("convT0", "convT1"), RD(f"scrq2_{hc}"))

    for i in range(32):
        startup_units.append(lambda i=i: _mu_w1(i))
    for i in range(32):
        startup_units.append(lambda i=i: _mu_w2(i))
    startup_units.append(lambda: _scales_w1())
    startup_units.append(lambda: _scales_w2())
    startup_units.extend(lambda hc=hc: _qu_w1(hc) for hc in range(HCN))
    startup_units.extend(lambda hc=hc: _qu_w2(hc) for hc in range(HCN))

    def drain(lst, n):
        for _ in range(min(n, len(lst))):
            lst.pop(0)()

    W1v = W1_h.rearrange("(kc p) e -> kc p e", p=128)

    def _w1prep(half, ec):
        buf = qldA if ec % 2 == 0 else qldB
        bk = "oacc0" if ec % 2 == 0 else "oacc1"
        slab = slW1[:, half * KC * 128:(half + 1) * KC * 128]
        slk = f"slW1_{half}"
        em.dma("qs", lambda E, ec=ec, buf=buf: E.dma_start(
            out=buf[:].rearrange("p (kc c) -> p kc c", c=128),
            in_=W1v[:, :, ec * 128:(ec + 1) * 128].rearrange("kc p c -> p kc c")),
            (), RD(bk))
        em.op("gp", lambda E, buf=buf, slab=slab: E.tensor_copy(slab, buf[:]),
              RD(bk), RD(slk))
        em.dma("qs", lambda E, ec=ec, slab=slab: E.dma_start(out=scr_W1[ec], in_=slab),
               RD(slk), RD(f"scrW1_{ec}"))

    w1prep_units = [(half, ec) for j0 in range(KC) for half, ec in ((0, 8 + j0), (1, j0))]

    def producer(p):
        xr = xring[:, (p % 2) * D:(p % 2 + 1) * D]
        xrk = f"xr{p % 2}"
        xn = xnring[:, (p % 2) * D:(p % 2 + 1) * D]
        xnk = f"xn{p % 2}"
        xb = xnbf[:, (p % 2) * D:(p % 2 + 1) * D]
        xbk = f"xb{p % 2}"
        nb = nbf[:, (p % 2) * D:(p % 2 + 1) * D]
        nbk = f"nb{p % 2}"
        em.dma("qs", lambda E, p=p: E.dma_start(out=xr, in_=x_h[p * 128:(p + 1) * 128, :]),
               (), RD(xrk))
        em.op("dve", lambda E: E.reduce_sum(colk[:, 2:3], xr, AX.X), RD(xrk), RD("pc2"))
        em.op("act", lambda E: E.activation(pt2[:], xr, Act.Square, accum_out=colk[:, 3:4]),
              RD(xrk), RD("pt2", "pc3"))
        em.op("dve", lambda E: E.tensor_scalar_mul(colk[:, 2:3], colk[:, 2:3], 1.0 / D),
              RD("pc2"), RD("pc2"))
        em.op("dve", lambda E: E.tensor_scalar_mul(colk[:, 3:4], colk[:, 3:4], 1.0 / D),
              RD("pc3"), RD("pc3"))
        em.op("dve", lambda E: E.tensor_mul(colk[:, 4:5], colk[:, 2:3], colk[:, 2:3]),
              RD("pc2"), RD("pc4"))
        em.op("dve", lambda E: E.tensor_sub(colk[:, 4:5], colk[:, 3:4], colk[:, 4:5]),
              RD("pc3", "pc4"), RD("pc4"))
        em.op("dve", lambda E: E.tensor_scalar_add(colk[:, 4:5], colk[:, 4:5], 1e-5),
              RD("pc4"), RD("pc4"))
        em.op("act", lambda E: E.activation(colk[:, 5:6], colk[:, 4:5], Act.Sqrt),
              RD("pc4"), RD("pc5"))
        em.op("dve", lambda E: E.reciprocal(colk[:, 5:6], colk[:, 5:6]), RD("pc5"), RD("pc5"))
        for _ in range(2):
            em.op("dve", lambda E: E.tensor_mul(colk[:, 6:7], colk[:, 5:6], colk[:, 5:6]),
                  RD("pc5"), RD("pc6"))
            em.op("dve", lambda E: E.tensor_mul(colk[:, 6:7], colk[:, 6:7], colk[:, 4:5]),
                  RD("pc6", "pc4"), RD("pc6"))
            em.op("dve", lambda E: E.tensor_scalar(colk[:, 6:7], colk[:, 6:7], -0.5, 1.5,
                                                   Alu.mult, Alu.add), RD("pc6"), RD("pc6"))
            em.op("dve", lambda E: E.tensor_mul(colk[:, 5:6], colk[:, 5:6], colk[:, 6:7]),
                  RD("pc5", "pc6"), RD("pc5"))
        em.op("dve", lambda E: E.tensor_scalar(xn, xr, colk[:, 2:3], colk[:, 5:6],
                                               Alu.subtract, Alu.mult),
              RD(xrk, "pc2", "pc5"), RD(xnk))
        em.op("dve", lambda E: E.tensor_copy(xb, xn), RD(xnk), RD(xbk))
        em.op("dve", lambda E: E.reduce_max(colk[:, 7:8], xn, AX.X, apply_absolute_value=True),
              RD(xnk), RD("pc7"))
        em.op("dve", lambda E: E.tensor_single_scalar(colk[:, 7:8], colk[:, 7:8], 1e-5, Alu.max),
              RD("pc7"), RD("pc7"))
        em.op("dve", lambda E: E.reciprocal(colk[:, 8:9], colk[:, 7:8]), RD("pc7"), RD("pc8"))
        em.op("dve", lambda E: E.tensor_scalar_mul(colk[:, 8:9], colk[:, 8:9], 127.0),
              RD("pc8"), RD("pc8"))
        em.op("dve", lambda E: E.tensor_scalar_mul(colk[:, 9:10], colk[:, 7:8], -1.0 / 127.0),
              RD("pc7"), RD("pc9"))
        em.op("act", lambda E: E.activation(pt1[:], xn, Act.Copy, bias=0.5, scale=colk[:, 8:9]),
              RD(xnk, "pc8"), RD("pt1"))
        em.op("act", lambda E: E.activation(pti[:], pt1[:], Act.Copy), RD("pt1"), RD("pti"))
        em.op("dve", lambda E: E.tensor_sub(pt2[:], pt1[:], pti[:]), RD("pt1", "pti"), RD("pt2"))
        em.op("dve", lambda E: E.scalar_tensor_tensor(pt1[:], pt2[:], 1.0, pti[:], Alu.is_ge, Alu.add),
              RD("pt2", "pti"), RD("pt1"))
        em.op("dve", lambda E: E.scalar_tensor_tensor(pt2[:], pt2[:], 0.0, pt1[:], Alu.is_lt, Alu.subtract),
              RD("pt2", "pt1"), RD("pt2"))
        em.op("dve", lambda E: E.tensor_scalar_mul(nb, pt2[:], colk[:, 9:10]),
              RD("pt2", "pc9"), RD(nbk))
        g_of = (p - 1) // 4
        j = (p - 1) % 4
        if g_of < NG:
            par = g_of % 2
            for kc in range(KC):
                dst = xnT[:, par * KC * BUFW + kc * BUFW + 16 + j * 128:
                          par * KC * BUFW + kc * BUFW + 16 + (j + 1) * 128]
                slotA = pbT[:, 768:896]
                em.op("pe", lambda E, slotA=slotA, kc=kc, xb=xb: E.transpose(
                    slotA, xb[:, kc * 128:(kc + 1) * 128], idb[:]),
                    RD(xbk, "idb", "pbT"), RD("pbT"), tr=True)
                em.op("dve", lambda E, dst=dst, slotA=slotA: E.tensor_copy(dst, slotA),
                      RD("pbT"), RD(f"xnT{par}", "pbT"))
                dst2 = nT[:, par * KC * BUFW + kc * BUFW + 16 + j * 128:
                          par * KC * BUFW + kc * BUFW + 16 + (j + 1) * 128]
                slotB = pbT[:, 896:1024]
                em.op("pe", lambda E, slotB=slotB, kc=kc, nb=nb: E.transpose(
                    slotB, nb[:, kc * 128:(kc + 1) * 128], idb[:]),
                    RD(nbk, "idb", "pbT"), RD("pbT"), tr=True)
                em.op("dve", lambda E, dst2=dst2, slotB=slotB: E.tensor_copy(dst2, slotB),
                      RD("pbT"), RD(f"nT{par}", "pbT"))
        if j == 0 and g_of >= 1 and g_of - 1 < NG:
            parh = (g_of - 1) % 2
            par0 = g_of % 2
            if g_of < NG:
                xv = xnT[:].rearrange("p (q kc c) -> p q kc c", q=2, c=BUFW)
                nv = nT[:].rearrange("p (q kc c) -> p q kc c", q=2, c=BUFW)
                em.op("dve", lambda E, xv=xv: E.tensor_copy(
                    xv[:, parh, :, 528:544], xv[:, par0, :, 16:32]),
                    RD(f"xnT{par0}"), RD(f"xnT{parh}"))
                em.op("dve", lambda E, nv=nv: E.tensor_copy(
                    nv[:, parh, :, 528:544], nv[:, par0, :, 16:32]),
                    RD(f"nT{par0}"), RD(f"nT{parh}"))
            else:
                for kc in range(KC):
                    dsth = xnT[:, parh * KC * BUFW + kc * BUFW + 528:
                               parh * KC * BUFW + kc * BUFW + 544]
                    slotA = pbT[:, 768:896]
                    em.op("pe", lambda E, slotA=slotA, kc=kc, xb=xb: E.transpose(
                        slotA, xb[:, kc * 128:(kc + 1) * 128], idb[:]),
                        RD(xbk, "idb", "pbT"), RD("pbT"), tr=True)
                    em.op("dve", lambda E, dsth=dsth, slotA=slotA: E.tensor_copy(
                        dsth, slotA[:, 0:16]), RD("pbT"), RD(f"xnT{parh}", "pbT"))
                    dsth2 = nT[:, parh * KC * BUFW + kc * BUFW + 528:
                               parh * KC * BUFW + kc * BUFW + 544]
                    slotB = pbT[:, 896:1024]
                    em.op("pe", lambda E, slotB=slotB, kc=kc, nb=nb: E.transpose(
                        slotB, nb[:, kc * 128:(kc + 1) * 128], idb[:]),
                        RD(nbk, "idb", "pbT"), RD("pbT"), tr=True)
                    em.op("dve", lambda E, dsth2=dsth2, slotB=slotB: E.tensor_copy(
                        dsth2, slotB[:, 0:16]), RD("pbT"), RD(f"nT{parh}", "pbT"))

    def halo_left(g):
        par = g % 2
        if g == 0:
            for kc in range(KC):
                em.op("dve", lambda E, kc=kc: E.memset(
                    xnT[:, kc * BUFW:kc * BUFW + 16], 0.0), (), RD("xnT0"))
                em.op("dve", lambda E, kc=kc: E.memset(
                    nT[:, kc * BUFW:kc * BUFW + 16], 0.0), (), RD("nT0"))
            return
        spar = (g - 1) % 2
        for kc in range(KC):
            em.op("dve", lambda E, kc=kc: E.tensor_copy(
                xnT[:, par * KC * BUFW + kc * BUFW:par * KC * BUFW + kc * BUFW + 16],
                xnT[:, spar * KC * BUFW + kc * BUFW + 512:spar * KC * BUFW + kc * BUFW + 528]),
                RD(f"xnT{spar}"), RD(f"xnT{par}"))
            em.op("dve", lambda E, kc=kc: E.tensor_copy(
                nT[:, par * KC * BUFW + kc * BUFW:par * KC * BUFW + kc * BUFW + 16],
                nT[:, spar * KC * BUFW + kc * BUFW + 512:spar * KC * BUFW + kc * BUFW + 528]),
                RD(f"nT{spar}"), RD(f"nT{par}"))

    def attn_W1(g, cb=None):
        par = g % 2
        xnTk = f"xnT{par}"
        base = par * KC * BUFW
        for j in range(KC):
            for half, ec in ((0, 8 + j), (1, j)):
                bank = pb[half]
                bkey = f"pb{half}"
                slab = slW1[:, half * KC * 128:(half + 1) * KC * 128]
                slk = f"slW1_{half}"
                if g > 0:
                    em.dma("qs", lambda E, ec=ec, slab=slab: E.dma_start(out=slab, in_=scr_W1[ec]),
                           RD(f"scrW1_{ec}"), RD(slk))
                else:
                    _w1prep(*w1prep_units.pop(0))
                bslot = pb[2][:, (ec % 16) * 32:(ec % 16) * 32 + 16]
                bskey = "pb2"
                for kc in range(KC):
                    lhs = slab[:, kc * 128:(kc + 1) * 128]
                    rhsA = xnT[:, base + kc * BUFW + 8:base + kc * BUFW + 520]
                    rhsB = xnT[:, base + kc * BUFW + 520:base + kc * BUFW + 536]
                    em.op("pe", lambda E, lhs=lhs, rhsA=rhsA, bank=bank, kc=kc:
                          E.matmul(bank[:, 0:512], lhs, rhsA, start=(kc == 0), stop=(kc == KC - 1)),
                          RD(slk, xnTk), RD(bkey))
                    em.op("pe", lambda E, lhs=lhs, rhsB=rhsB, bslot=bslot, kc=kc:
                          E.matmul(bslot, lhs, rhsB, start=(kc == 0), stop=(kc == KC - 1)),
                          RD(slk, xnTk), RD(bskey))
            gb = pb[0]; gbk = "pb0"
            ab = pb[1]; abk = "pb1"
            gbs = pb[2][:, ((8 + j) % 16) * 32:((8 + j) % 16) * 32 + 16]; gbsk = "pb2"
            abs_ = pb[2][:, (j % 16) * 32:(j % 16) * 32 + 16]; absk = "pb2"
            sgj = sg[:, (j % 2) * AWIN:(j % 2 + 1) * AWIN]
            sgk = f"sg{j % 2}"
            xgj = xglu[:, j * AWIN:(j + 1) * AWIN]
            xgk = f"xglu{j}"
            em.op("act", lambda E, sgj=sgj, gb=gb, j=j: E.activation(
                sgj[:, 0:512], gb[:, 0:512], Act.Sigmoid, bias=b1g[:, j:j + 1]),
                RD(gbk, "b1g"), RD(sgk))
            em.op("act", lambda E, sgj=sgj, gbs=gbs, j=j: E.activation(
                sgj[:, 512:528], gbs, Act.Sigmoid, bias=b1g[:, j:j + 1]),
                RD(gbsk, "b1g"), RD(sgk))
            em.op("dve", lambda E, xgj=xgj, ab=ab, sgj=sgj, j=j: E.scalar_tensor_tensor(
                xgj[:, 0:512], ab[:, 0:512], b1a[:, j:j + 1], sgj[:, 0:512], Alu.add, Alu.mult),
                RD(abk, "b1a", sgk), RD(xgk))
            em.op("dve", lambda E, xgj=xgj, abs_=abs_, sgj=sgj, j=j: E.scalar_tensor_tensor(
                xgj[:, 512:528], abs_, b1a[:, j:j + 1], sgj[:, 512:528], Alu.add, Alu.mult),
                RD(absk, "b1a", sgk), RD(xgk))
            if g == 0:
                em.op("dve", lambda E, xgj=xgj: E.memset(xgj[:, 0:8], 0.0), (), RD(xgk))
            if cb is not None:
                cb(j)

    def attn_rest(g):
        for j in range(KC):
            xgj = xglu[:, j * AWIN:(j + 1) * AWIN]
            xgk = f"xglu{j}"
            dj = dsc[:, (j % 2) * 15 * 128:(j % 2 + 1) * 15 * 128]
            djk = f"dsc{j % 2}"
            em.dma("qg", lambda E, j=j, dj=dj: E.dma_start(out=dj, in_=scr_dg[j]),
                   RD(f"scrdg_{j}"), RD(djk))
            cbank = pb[j % 2]
            cbk = f"pb{j % 2}"
            for k in range(15):
                em.op("pe", lambda E, dj=dj, xgj=xgj, k=k, cbank=cbank: E.matmul(
                    cbank[:, 0:512], dj[:, k * 128:(k + 1) * 128], xgj[:, k + 1:k + 1 + TG],
                    start=(k == 0), stop=(k == 14)), RD(djk, xgk), RD(cbk))
            cT = convT[:, j * TG:(j + 1) * TG]
            em.op("dve", lambda E, cT=cT, cbank=cbank: E.tensor_copy(cT, cbank[:, 0:512]),
                  RD(cbk), RD(f"convT{j}"))
        for pair in range(2):
            for stj2 in range(2):
                stj = 2 * pair + stj2
                xo = xout[:, 0:D]
                xok = "xout0"
                u = PADL + g * TG + stj * 128
                em.dma("qs", lambda E, u=u, xo=xo, stj=stj: E.dma_start(out=xo, in_=x_h[u:u + 128, :]),
                       (), RD(xok)) if stj2 == 99 else None
            for kc in range(KC):
                slab = slW2[:, (kc % 2) * D:(kc % 2 + 1) * D]
                slk = f"slW2_{kc % 2}"
                em.dma("qg", lambda E, kc=kc, slab=slab: E.dma_start(out=slab, in_=scr_W2[kc]),
                       RD(f"scrW2_{kc}"), RD(slk))
                for stj2 in range(2):
                    stj = 2 * pair + stj2
                    bk0 = pb[0] if stj2 == 0 else pb[2]
                    bk1 = pb[1] if stj2 == 0 else pb[6]
                    b0key = "pb0" if stj2 == 0 else "pb2"
                    b1key = "pb1" if stj2 == 0 else "pb6"
                    lhs = convT[:, kc * TG + stj * 128:kc * TG + (stj + 1) * 128]
                    em.op("pe", lambda E, lhs=lhs, bk0=bk0, slab=slab, kc=kc: E.matmul(
                        bk0[:, 0:512], lhs, slab[:, 0:512],
                        start=(kc == 0), stop=(kc == KC - 1)),
                        RD(f"convT{kc}", slk), RD(b0key))
                    em.op("pe", lambda E, lhs=lhs, bk1=bk1, slab=slab, kc=kc: E.matmul(
                        bk1[:, 0:512], lhs, slab[:, 512:1024],
                        start=(kc == 0), stop=(kc == KC - 1)),
                        RD(f"convT{kc}", slk), RD(b1key))
            for stj2 in range(2):
                stj = 2 * pair + stj2
                xo = xout[:, 0:D]
                xok = "xout0"
                u = PADL + g * TG + stj * 128
                em.dma("qs", lambda E, u=u, xo=xo: E.dma_start(out=xo, in_=x_h[u:u + 128, :]),
                       (), RD(xok))
                em.op("gp", lambda E, xo=xo: E.tensor_add(xo, xo, b2b[:]), RD(xok, "b2b"), RD(xok))
                bk0 = pb[0] if stj2 == 0 else pb[2]
                bk1 = pb[1] if stj2 == 0 else pb[6]
                b0key = "pb0" if stj2 == 0 else "pb2"
                b1key = "pb1" if stj2 == 0 else "pb6"
                oa = out_acc[:, stj * D:(stj + 1) * D]
                oak = f"oacc{stj}"
                em.op("dve", lambda E, oa=oa, bk0=bk0, xo=xo: E.tensor_add(oa[:, 0:512], bk0[:, 0:512],
                                                                           xo[:, 0:512]),
                      RD(b0key, xok), RD(oak))
                em.op("dve", lambda E, oa=oa, bk1=bk1, xo=xo: E.tensor_add(oa[:, 512:1024], bk1[:, 0:512],
                                                                           xo[:, 512:1024]),
                      RD(b1key, xok), RD(oak))
                if dbg_h is not None and g == 0:
                    em.dma("qs", lambda E, stj=stj, oa=oa: E.dma_start(
                        out=dbg_h[stj * 128:(stj + 1) * 128, :], in_=oa), RD(oak), RD(f"dbg{stj}"))

    def ffn1_half(g, hh, ucb=None):
        par = g % 2
        nTk = f"nT{par}"
        base = par * KC * BUFW
        woff = 16 + hh * HALF - 2
        for hc in range(HCN):
            if ucb is not None:
                ucb(hc)
            slab = slq1[:, (hc % 2) * KC * 128:(hc % 2 + 1) * KC * 128]
            slk = f"slq1_{hc % 2}"
            em.dma("qg", lambda E, hc=hc, slab=slab: E.dma_start(out=slab, in_=scr_q1[hc]),
                   RD(f"scrq1_{hc}"), RD(slk))
            bank = pb[3 + hc % 3]
            bkey = f"pb{3 + hc % 3}"
            for kc in range(KC):
                lhs = slab[:, kc * 128:(kc + 1) * 128]
                rhs = nT[:, base + kc * BUFW + woff:base + kc * BUFW + woff + FWIN]
                em.op("pe", lambda E, lhs=lhs, rhs=rhs, bank=bank, kc=kc: E.matmul(
                    bank[:, 0:FWIN], lhs, rhs, start=(kc == 0), stop=(kc == KC - 1)),
                    RD(slk, nTk), RD(bkey))
            Aap = cvA[:, (hc % 2) * HALF:(hc % 2 + 1) * HALF]
            Ak = f"cvA{hc % 2}"
            Bap = cvB[:, (hc % 2) * HALF:(hc % 2 + 1) * HALF]
            Bk = f"cvB{hc % 2}"
            Yap = cvY[:, (hc % 2) * HALF:(hc % 2 + 1) * HALF]
            Yk = f"cvY{hc % 2}"
            w0 = cwc[:, hc * 3 + 0:hc * 3 + 1]
            w1c = cwc[:, hc * 3 + 1:hc * 3 + 2]
            w2c = cwc[:, hc * 3 + 2:hc * 3 + 3]
            cb = cbc[:, hc:hc + 1]
            em.op("dve", lambda E, Aap=Aap, bank=bank, w1c=w1c, cb=cb: E.tensor_scalar(
                Aap, bank[:, 2:2 + HALF], w1c, cb, Alu.mult, Alu.add),
                RD(bkey, "cwc", "cbc"), RD(Ak))
            em.op("dve", lambda E, Bap=Bap, bank=bank, w0=w0, Aap=Aap: E.scalar_tensor_tensor(
                Bap, bank[:, 1:1 + HALF], w0, Aap, Alu.mult, Alu.add),
                RD(bkey, "cwc", Ak), RD(Bk))
            em.op("dve", lambda E, Yap=Yap, bank=bank, w2c=w2c, Bap=Bap: E.scalar_tensor_tensor(
                Yap, bank[:, 3:3 + HALF], w2c, Bap, Alu.mult, Alu.add),
                RD(bkey, "cwc", Bk), RD(Yk))
            h2ap = h2[:, hc * HALF:(hc + 1) * HALF]
            em.op("act", lambda E, h2ap=h2ap, Yap=Yap: E.activation(h2ap, Yap, Act.Silu),
                  RD(Yk), RD(f"h2_{hc}"))
            if hc % 4 == 3:
                q4 = hc - 3
                h2b = h2[:, q4 * HALF:(q4 + 4) * HALF]
                h2keys4 = tuple(f"h2_{i}" for i in range(q4, q4 + 4))
                for op, store, storek in ((Alu.max, Mhi, "Mhi"), (Alu.min, sB, "sB")):
                    em.op("dve", lambda E, op=op, h2b=h2b: E.tensor_tensor(
                        Mbig[:, 0:512], h2b[:, 0:512], h2b[:, 512:1024], op), h2keys4, RD("Mbig"))
                    em.op("dve", lambda E, op=op: E.tensor_tensor(
                        Mbig[:, 512:768], Mbig[:, 0:256], Mbig[:, 256:512], op), RD("Mbig"), RD("Mbig"))
                    if hc == 3:
                        em.op("dve", lambda E, store=store: E.tensor_copy(store[:], Mbig[:, 512:768]),
                              RD("Mbig"), RD(storek))
                    else:
                        em.op("dve", lambda E, op=op, store=store: E.tensor_tensor(
                            store[:], store[:], Mbig[:, 512:768], op), RD("Mbig", storek), RD(storek))

    def ffn2_half(g, hh, ucb=None):
        par = g % 2
        em.op("dve", lambda E: E.tensor_scalar_mul(sB[:], sB[:], -1.0), RD("sB"), RD("sB"))
        em.op("dve", lambda E: E.tensor_max(sB[:], sB[:], Mhi[:]), RD("sB", "Mhi"), RD("sB"))
        split3(sB[:], HALF, shi[:], smid[:], slo[:], st1[:], st2[:],
               "sB", ("shi", "smid", "slo"), ("st1", "st2"))
        for blk in range(2):
            for ki, (sp, spk) in enumerate(((shi, "shi"), (smid, "smid"), (slo, "slo"))):
                em.op("pe", lambda E, sp=sp, blk=blk, ki=ki: E.transpose(
                    pbT[:, ki * 128:(ki + 1) * 128], sp[:, blk * 128:(blk + 1) * 128], idb[:]),
                    RD(spk, "idb"), RD("pbT"), tr=True)
            em.op("dve", lambda E: E.tensor_copy(rc[:], pbT[:, 0:128]), RD("pbT"), RD("rc"))
            em.op("dve", lambda E: E.tensor_add(rc[:], rc[:], pbT[:, 128:256]), RD("pbT", "rc"), RD("rc"))
            em.op("dve", lambda E: E.tensor_add(rc[:], rc[:], pbT[:, 256:384]), RD("pbT", "rc"), RD("rc"))
            em.op("dve", lambda E, blk=blk: E.reduce_max(scol[:, blk:blk + 1], rc[:], AX.X,
                                                         apply_absolute_value=True),
                  RD("rc"), RD(f"scol{blk}"))
        for blk in range(2):
            em.op("dve", lambda E, blk=blk: E.tensor_single_scalar(
                scol[:, blk:blk + 1], scol[:, blk:blk + 1], 1e-5, Alu.max),
                RD(f"scol{blk}"), RD(f"scol{blk}"))
            em.op("dve", lambda E, blk=blk: E.tensor_scalar_mul(
                scol[:, 2 + blk:3 + blk], scol[:, blk:blk + 1], -1.0 / 127.0),
                RD(f"scol{blk}"), RD(f"ecol{blk}"))
            em.op("dve", lambda E, blk=blk: E.tensor_mul(
                scol[:, 2 + blk:3 + blk], scol[:, 2 + blk:3 + blk], qabs[:, 35:36]),
                RD(f"ecol{blk}", "qab35"), RD(f"ecol{blk}"))
            em.op("dve", lambda E, blk=blk: E.reciprocal(scol[:, 4 + blk:5 + blk],
                                                         scol[:, blk:blk + 1]),
                  RD(f"scol{blk}"), RD(f"s2c{blk}"))
            em.op("dve", lambda E, blk=blk: E.tensor_scalar_mul(
                scol[:, 4 + blk:5 + blk], scol[:, 4 + blk:5 + blk], 127.0),
                RD(f"s2c{blk}"), RD(f"s2c{blk}"))
        em.op("dve", lambda E: E.tensor_copy(scol[:, 6:8], scol[:, 4:6]), RD("s2c0", "s2c1"), RD("s2cat"))
        split3(scol[:, 6:8], 2, rsp[:, 0:2], rsp[:, 2:4], rsp[:, 4:6], rspt[:, 0:2], rspt[:, 2:4],
               "s2cat", ("rsp", "rsp", "rsp"), ("rsp", "rsp"))
        for ki in range(3):
            for blk in range(2):
                sl = (2 * ki + blk) * 128
                em.op("pe", lambda E, sl=sl, ki=ki, blk=blk: E.transpose(
                    pbT[0:1, sl:sl + 128],
                    rsp[:, 2 * ki + blk:2 * ki + blk + 1], idb[:]),
                    RD("rsp", "idb"), RD("pbT"), tr=True)
        for ki in range(3):
            for blk in range(2):
                sl = (2 * ki + blk) * 128
                em.op("dve", lambda E, sl=sl, ki=ki, blk=blk: E.tensor_copy(
                    Srow[0:1, ki * HALF + blk * 128:ki * HALF + (blk + 1) * 128],
                    pbT[0:1, sl:sl + 128]),
                    RD("pbT", "Srow_z"), RD("SrowV"))
        for ki in range(3):
            em.op("pe", lambda E, ki=ki: E.matmul(pb[6][:, 0:HALF], onesb[:],
                                                  Srow[:, ki * HALF:(ki + 1) * HALF],
                                                  start=(ki == 0), stop=(ki == 2)),
                  RD("SrowV", "Srow_z", "onesb"), RD("pb6"))
        em.op("dve", lambda E: E.tensor_copy(s2b[:], pb[6][:, 0:HALF]), RD("pb6"), RD("s2b"))
        if dbg2_h is not None and g == 0 and hh == 0:
            em.dma("qs", lambda E: E.dma_start(out=dbg2_h[:], in_=h2[:]),
                   tuple(f"h2_{i}" for i in range(HCN)), RD("dbg2"))
            em.dma("qs", lambda E: E.dma_start(out=dbg3_h[:, 0:HALF], in_=s2b[:]),
                   RD("s2b"), RD("dbg3"))
            em.dma("qs", lambda E: E.dma_start(out=dbg3_h[:, HALF:HALF + 4], in_=scol[:, 2:6]),
                   RD("ecol0", "ecol1", "s2c0", "s2c1"), RD("dbg3"))
            em.dma("qs", lambda E: E.dma_start(out=dbg4_h[:, 0:KC * TG].bitcast(BF16), in_=convT[:]),
                   tuple(f"convT{j}" for j in range(KC)), RD("dbg4"))
        for q in range(KC):
            if ucb is not None:
                ucb(q)
            hcs = range(4 * q, 4 * q + 4)
            h2b = h2[:, 4 * q * HALF:(4 * q + 4) * HALF]
            h2keys = tuple(f"h2_{i}" for i in hcs)
            v = ht1[:, 0:4 * HALF]
            vk = "ht1"
            em.op("dve", lambda E, v=v, h2b=h2b: E.tensor_tensor(
                v.rearrange("p (c t) -> p c t", t=HALF),
                h2b.rearrange("p (c t) -> p c t", t=HALF),
                s2b[:, None, :].to_broadcast([128, 4, HALF]), Alu.mult),
                h2keys + ("s2b",), RD(vk))
            em.op("act", lambda E, v=v: E.activation(pti[:], v, Act.Copy, bias=0.5),
                  RD(vk), RD("pti"))
            em.op("dve", lambda E, v=v: E.scalar_tensor_tensor(
                htc[:], v, 0.5, pti[:], Alu.add, Alu.subtract), RD(vk, "pti"), RD("htc"))
            em.op("dve", lambda E: E.scalar_tensor_tensor(
                htd[:], htc[:], 1.0, pti[:], Alu.is_ge, Alu.add), RD("htc", "pti"), RD("htd"))
            n2a = n2T[:, (q % 2) * 4 * HALF:(q % 2 + 1) * 4 * HALF]
            n2k = f"n2T{q % 2}"
            em.op("dve", lambda E, n2a=n2a: E.scalar_tensor_tensor(
                n2a, htc[:], 0.0, htd[:], Alu.is_lt, Alu.subtract), RD("htc", "htd"), RD(n2k))
            for ii, hc in enumerate(hcs):
                slab2 = slq2[:, (hc % 2) * D:(hc % 2 + 1) * D]
                sl2k = f"slq2_{hc % 2}"
                em.dma("qg", lambda E, hc=hc, slab2=slab2: E.dma_start(out=slab2, in_=scr_q2[hc]),
                       RD(f"scrq2_{hc}"), RD(sl2k))
                for stj in range(2):
                    bA = pb[3 + 2 * stj]
                    bAk = f"pb{3 + 2 * stj}"
                    bB = pb[4 + 2 * stj]
                    bBk = f"pb{4 + 2 * stj}"
                    lhs = n2a[:, ii * HALF + stj * 128:ii * HALF + (stj + 1) * 128]
                    em.op("pe", lambda E, lhs=lhs, bA=bA, slab2=slab2, hc=hc: E.matmul(
                        bA[:, 0:512], lhs, slab2[:, 0:512], start=(hc == 0), stop=(hc == HCN - 1)),
                        RD(n2k, sl2k), RD(bAk))
                    em.op("pe", lambda E, lhs=lhs, bB=bB, slab2=slab2, hc=hc: E.matmul(
                        bB[:, 0:512], lhs, slab2[:, 512:1024], start=(hc == 0), stop=(hc == HCN - 1)),
                        RD(n2k, sl2k), RD(bBk))
        for stj in range(2):
            sub = hh * 2 + stj
            oa = out_acc[:, sub * D:(sub + 1) * D]
            oak = f"oacc{sub}"
            ecol = scol[:, 2 + stj:3 + stj]
            eck = f"ecol{stj}"
            bA = pb[3 + 2 * stj]
            bAk = f"pb{3 + 2 * stj}"
            bB = pb[4 + 2 * stj]
            bBk = f"pb{4 + 2 * stj}"
            em.op("dve", lambda E, oa=oa, bA=bA, ecol=ecol: E.scalar_tensor_tensor(
                oa[:, 0:512], bA[:, 0:512], ecol, oa[:, 0:512], Alu.mult, Alu.add),
                RD(bAk, eck, oak), RD(oak))
            em.op("dve", lambda E, oa=oa, bB=bB, ecol=ecol: E.scalar_tensor_tensor(
                oa[:, 512:1024], bB[:, 0:512], ecol, oa[:, 512:1024], Alu.mult, Alu.add),
                RD(bBk, eck, oak), RD(oak))
            em.dma("qs", lambda E, g=g, sub=sub, oa=oa: E.dma_start(
                out=out_h[g * TG + sub * 128:g * TG + (sub + 1) * 128, :], in_=oa),
                RD(oak), RD(f"outst_{g}_{sub}"))

    for p in range(1, min(4, NSUB) + 1):
        producer(p)
    halo_left(0)
    for p in range(5, min(8, NSUB) + 1):
        producer(p)
    attn_W1(0, cb=lambda j: drain(startup_units, 17))
    drain(startup_units, 10 ** 9)
    for g in range(NG):
        attn_rest(g)
        ffn1_half(g, 0)
        ffn2_half(g, 0)
        ffn1_half(g, 1)
        if g + 1 < NG:
            halo_left(g + 1)
            if 4 * g + 9 <= NSUB:
                producer(4 * g + 9)
            attn_W1(g + 1)
            for p in range(4 * g + 10, 4 * g + 13):
                if p <= NSUB:
                    producer(p)
        ffn2_half(g, 1)

    need = em.barrier_token([f"outst_{g}_{sub}" for g in range(NG) for sub in range(4)])
    for s, v in sorted(need.items()):
        em.ops["gp"].append(("w", s, v))

    run = _mk_closure_runner(em)
    with nc.Block() as block:
        @block.sync
        def _(e):
            run("sq", e)

        @block.vector
        def _(e):
            run("dve", e)

        @block.scalar
        def _(e):
            run("act", e)

        @block.gpsimd
        def _(e):
            run("gp", e)

        @block.tensor
        def _(e):
            run("pe", e)
    ctx.close()
    return nc


def prep_inputs(L0_attn_inputs, L0_attn_W1, L0_attn_b1, L0_attn_W2, L0_attn_b2,
                L0_attn_conv_shared_weight, L0_ffn_w1, L0_ffn_conv_w, L0_ffn_conv_b,
                L0_ffn_w2):
    x = np.asarray(L0_attn_inputs, dtype=np.float32)
    B, S, _ = x.shape
    W1 = np.ascontiguousarray(np.asarray(L0_attn_W1, np.float32))
    b1 = np.asarray(L0_attn_b1, np.float32)
    W2 = np.ascontiguousarray(np.asarray(L0_attn_W2, np.float32))
    b2 = np.asarray(L0_attn_b2, np.float32)
    csw = np.asarray(L0_attn_conv_shared_weight, np.float32)
    fw1 = np.ascontiguousarray(np.asarray(L0_ffn_w1, np.float32))
    fcw = np.asarray(L0_ffn_conv_w, np.float32)
    fcb = np.asarray(L0_ffn_conv_b, np.float32)
    fw2 = np.ascontiguousarray(np.asarray(L0_ffn_w2, np.float32))

    b1a = np.ascontiguousarray(b1[:D].reshape(8, 128).T)
    b1g = np.ascontiguousarray(b1[D:].reshape(8, 128).T)
    b2b = np.ascontiguousarray(np.broadcast_to(b2[None, :], (128, D)))
    ident = np.eye(128, dtype=np.float32).astype(ml_dtypes.bfloat16)

    def csw_rep(flip):
        c = csw[:, ::-1] if flip else csw
        out = np.zeros((128, 8 * 15), np.float32)
        for j in range(8):
            out[0:64, j * 15:(j + 1) * 15] = c[2 * j]
            out[64:128, j * 15:(j + 1) * 15] = c[2 * j + 1]
        return out

    def cw_rep(flip):
        c = fcw[:, 0, ::-1] if flip else fcw[:, 0, :]
        return np.ascontiguousarray(c.reshape(HCN, 128, 3).transpose(1, 0, 2).reshape(128, HCN * 3))

    cb_rs = np.ascontiguousarray(fcb.reshape(HCN, 128).T)
    fw1t = np.ascontiguousarray(fw1.T)
    fw2t = np.ascontiguousarray(fw2.T)

    in_maps = []
    for c in range(8):
        b, h = c // 2, c % 2
        xb = x[b]
        if h == 1:
            xb = xb[::-1]
        xpad = np.zeros((IN_T, D), np.float32)
        xpad[PADL:PADL + CORE_T] = xb[0:CORE_T]
        xpad[PADL + CORE_T:] = xb[CORE_T:CORE_T + HALO]
        in_maps.append({
            "xin": np.ascontiguousarray(xpad),
            "W1": W1, "b1a": b1a, "b1g": b1g, "W2": W2, "b2b": b2b,
            "cswr": np.ascontiguousarray(csw_rep(h == 1)),
            "fw1t": fw1t, "fcw": cw_rep(h == 1), "fcb": cb_rs, "fw2t": fw2t,
            "identb": ident,
        })
    return in_maps


def kernel(L0_attn_inputs, L0_attn_W1, L0_attn_b1, L0_attn_W2, L0_attn_b2,
           L0_attn_conv_shared_weight, L0_ffn_w1, L0_ffn_conv_w, L0_ffn_conv_b,
           L0_ffn_w2):
    in_maps = prep_inputs(L0_attn_inputs, L0_attn_W1, L0_attn_b1, L0_attn_W2,
                          L0_attn_b2, L0_attn_conv_shared_weight, L0_ffn_w1,
                          L0_ffn_conv_w, L0_ffn_conv_b, L0_ffn_w2)
    nc = build_nc()
    res = run_bass_kernel_spmd(nc, in_maps, core_ids=list(range(8)))
    x = np.asarray(L0_attn_inputs)
    B, S, _ = x.shape
    out = np.empty((B, S, D), np.float32)
    for c in range(8):
        b, h = c // 2, c % 2
        oc = np.asarray(res.results[c]["out"])
        if h == 0:
            out[b, 0:CORE_T] = oc
        else:
            out[b, CORE_T:] = oc[::-1]
    return out


# version 21
# speedup vs baseline: 1.0290x; 1.0290x over previous
import numpy as np
import ml_dtypes
import concourse.bass as bass
import concourse.mybir as mybir
from concourse.bass_utils import run_bass_kernel_spmd

F32 = mybir.dt.float32
BF16 = mybir.dt.bfloat16
I32 = mybir.dt.int32
Alu = mybir.AluOpType
Act = mybir.ActivationFunctionType
AX = mybir.AxisListType

D = 1024
E = 2048
HID = 4096
KC = 8
ECN = 16
HCN = 32
TG = 512
HALF = 256
NSUB_G = 4
CORE_T = 4096
PADL = 128
HALO = 128
IN_T = PADL + CORE_T + HALO
BUFW = 16 + TG + 16
AWIN = 528
FWIN = 260

ENGS = ("pe", "dve", "act", "gp", "sq")
DMAQ = {"qs": "sq", "qa": "act", "qg": "gp"}
NSEMQ = 6


class Em:

    def __init__(self, nc, self_waits=False):
        self.nc = nc
        self.self_waits = self_waits
        self.ops = {e: [] for e in ENGS}
        self.cnt = {e: 0 for e in ENGS}
        self.dcnt = {f"{q}{i}": 0 for q in DMAQ for i in range(NSEMQ)}
        self.didx = {q: 0 for q in DMAQ}
        self.waited = {s: {} for s in ENGS}
        self.reg = {}
        self.sem_handles = {}

    def _r(self, key):
        if key not in self.reg:
            self.reg[key] = {"w": None, "r": {}}
        return self.reg[key]

    def _edges(self, reads, writes):
        need = {}
        def add(tok):
            if tok is None:
                return
            s, v = tok
            if need.get(s, 0) < v:
                need[s] = v
        for k in reads:
            add(self._r(k)["w"])
        for k in writes:
            r = self._r(k)
            add(r["w"])
            for s, v in r["r"].items():
                add((s, v))
        return need

    def _emit_waits(self, stream, need, self_sem):
        for s, v in sorted(need.items()):
            if s == self_sem and s == "pe" and not self.self_waits:
                continue
            if self.waited[stream].get(s, 0) >= v:
                continue
            self.waited[stream][s] = v
            self.ops[stream].append(("w", s, v))

    def _update(self, reads, writes, tok):
        for k in reads:
            r = self._r(k)
            s, v = tok
            if r["r"].get(s, 0) < v:
                r["r"][s] = v
        for k in writes:
            r = self._r(k)
            r["w"] = tok
            r["r"] = {}

    def op(self, eng, fn, reads=(), writes=(), tr=False):
        need = self._edges(reads, writes)
        if eng == "pe" and tr != getattr(self, "pe_mode", False):
            self.pe_mode = tr
            if self.cnt["pe"] > 0 and not self.self_waits:
                self.ops["pe"].append(("w", "pe", self.cnt["pe"]))
                self.waited["pe"]["pe"] = self.cnt["pe"]
        self._emit_waits(eng, need, eng)
        self.cnt[eng] += 1
        tok = (eng, self.cnt[eng])
        self.ops[eng].append(("i", fn, eng))
        self._update(reads, writes, tok)
        return tok

    def dma(self, q, fn, reads=(), writes=()):
        stream = DMAQ[q]
        need = self._edges(reads, writes)
        self._emit_waits(stream, need, None)
        sem = f"{q}{self.didx[q] % NSEMQ}"
        self.didx[q] += 1
        if self.dcnt[sem] > 0:
            self._emit_waits(stream, {sem: self.dcnt[sem]}, None)
        self.dcnt[sem] += 16
        tok = (sem, self.dcnt[sem])
        self.ops[stream].append(("d", fn, sem))
        self._update(reads, writes, tok)
        return tok

    def barrier_token(self, keys):
        need = {}
        for k in keys:
            r = self._r(k)
            if r["w"] is not None:
                s, v = r["w"]
                if need.get(s, 0) < v:
                    need[s] = v
        return need


def _mk_closure_runner(em):
    def run(stream, E):
        sems = em.sem_handles
        for item in em.ops[stream]:
            if item[0] == "w":
                _, s, v = item
                E.wait_ge(sems[s], v)
            elif item[0] == "i":
                _, fn, eng = item
                fn(E).then_inc(sems[eng], 1)
            else:
                _, fn, sem = item
                fn(E).then_inc(sems[sem], 16)
    return run


def build_nc(n_groups=8, n_sub=None, debug=False, sim_safe=False):
    NG = n_groups
    NSUB = 4 * NG + 1 if n_sub is None else n_sub
    OUT_T = TG * NG

    nc = bass.Bass("TRN2", target_bir_lowering=False,
                   detect_race_conditions=sim_safe)
    ap = lambda name, shape, dt, kind: nc.dram_tensor(name, shape, dt, kind=kind).ap()
    x_h = ap("xin", [IN_T, D], F32, "ExternalInput")
    W1_h = ap("W1", [D, E], F32, "ExternalInput")
    b1a_h = ap("b1a", [128, 8], F32, "ExternalInput")
    b1g_h = ap("b1g", [128, 8], F32, "ExternalInput")
    W2_h = ap("W2", [D, D], F32, "ExternalInput")
    b2b_h = ap("b2b", [128, D], F32, "ExternalInput")
    csw_h = ap("cswr", [128, 8 * 15], F32, "ExternalInput")
    w1T_h = ap("fw1t", [D, HID], F32, "ExternalInput")
    cw_h = ap("fcw", [128, HCN * 3], F32, "ExternalInput")
    cb_h = ap("fcb", [128, HCN], F32, "ExternalInput")
    w2T_h = ap("fw2t", [HID, D], F32, "ExternalInput")
    idb_h = ap("identb", [128, 128], BF16, "ExternalInput")
    out_h = ap("out", [OUT_T, D], F32, "ExternalOutput")
    scr_W1 = ap("scr_W1", [ECN, 128, KC * 128], BF16, "Internal")
    scr_q1 = ap("scr_q1", [HCN, 128, KC * 128], BF16, "Internal")
    scr_q2 = ap("scr_q2", [HCN, 128, D], BF16, "Internal")
    scr_W2 = ap("scr_W2", [KC, 128, D], BF16, "Internal")
    scr_dg = ap("scr_dg", [KC, 128, 15 * 128], BF16, "Internal")
    dbg_h = ap("dbg", [NSUB_G * 128, D], F32, "ExternalOutput") if debug else None
    dbg2_h = ap("dbg2", [128, HCN * HALF], F32, "ExternalOutput") if debug else None
    dbg3_h = ap("dbg3", [128, HALF + 16], F32, "ExternalOutput") if debug else None
    dbg4_h = ap("dbg4", [128, KC * BUFW], BF16, "ExternalOutput") if debug else None

    em = Em(nc, self_waits=sim_safe)
    from contextlib import ExitStack
    ctx = ExitStack()
    sb = lambda name, shape, dt=F32: ctx.enter_context(nc.sbuf_tensor(name, shape, dt))
    psalloc = lambda name, shape, dt: ctx.enter_context(nc.psum_tensor(name, shape, dt))

    idb = sb("idb", [128, 128], BF16)
    onesb = sb("onesb", [128, 128], BF16)
    slW2 = sb("slW2", [128, 2 * D], BF16)
    slW1 = sb("slW1", [128, 2 * KC * 128], BF16)
    slq1 = sb("slq1", [128, 2 * KC * 128], BF16)
    slq2 = sb("slq2", [128, 2 * D], BF16)
    xnT = sb("xnT", [128, 2 * KC * BUFW], BF16)
    nT = sb("nT", [128, 2 * KC * BUFW], BF16)
    xring = sb("xring", [128, 2 * D])
    xnring = sb("xnring", [128, 2 * D])
    xnbf = sb("xnbf", [128, 2 * D], BF16)
    nbf = sb("nbf", [128, 2 * D], BF16)
    pt1 = sb("pt1", [128, D])
    pt2 = sb("pt2", [128, D])
    pti = ctx.enter_context(nc.sbuf_tensor("pti", [128, D], I32))
    h2 = sb("h2", [128, HCN * HALF])
    cvA = sb("cvA", [128, 2 * HALF])
    cvB = sb("cvB", [128, 2 * HALF])
    cvY = sb("cvY", [128, 2 * HALF])
    Mhi = sb("Mhi", [128, HALF])
    Mlo = sb("Mlo", [128, HALF])
    sB = sb("sB", [128, HALF])
    shi = sb("shi", [128, HALF], BF16)
    smid = sb("smid", [128, HALF], BF16)
    slo = sb("slo", [128, HALF], BF16)
    st1 = sb("st1", [128, HALF])
    st2 = sb("st2", [128, HALF])
    rc = sb("rc", [128, 128])
    scol = sb("scol", [128, 8])
    rsp = sb("rsp", [128, 2 * 3], BF16)
    rspt = sb("rspt", [128, 4])
    Srow = sb("Srow", [128, 3 * HALF], BF16)
    s2b = sb("s2b", [128, HALF])
    ht1 = sb("ht1", [128, 4 * HALF])
    hti = None
    htc = sb("htc", [128, 4 * HALF])
    htd = sb("htd", [128, 4 * HALF])
    n2T = sb("n2T", [128, 2 * 4 * HALF], BF16)
    sg = sb("sg", [128, 2 * AWIN])
    xglu = sb("xglu", [128, 8 * AWIN], BF16)
    dsc = sb("dsc", [128, 2 * 15 * 128], BF16)
    Mbig = sb("Mbig", [128, 1024])
    convT = sb("convT", [128, KC * TG], BF16)
    out_acc = sb("out_acc", [128, NSUB_G * D])
    xout = sb("xout", [128, D])
    b1a = sb("b1a_c", [128, 8])
    b1g = sb("b1g_c", [128, 8])
    b2b = sb("b2b_c", [128, D])
    cwc = sb("cw_c", [128, HCN * 3])
    cbc = sb("cb_c", [128, HCN])
    filt = sb("filt", [128, 8 * 15])
    fsc = sb("fsc", [128, 128])
    colk = sb("colk", [128, 16])
    qabs = sb("qabs", [128, 40])
    qldA = out_acc[:, 0:D]
    qldB = out_acc[:, D:2 * D]
    qv = out_acc[:, 2 * D:3 * D]
    qc = out_acc[:, 3 * D:4 * D]
    qw = convT[:, 0:D]
    qstg = convT[:, D:2 * D]

    pb = [psalloc(f"pb{i}", [128, 512], F32) for i in range(7)]
    pbT = psalloc("pbT", [128, 1024], BF16)

    for s in list(ENGS) + [f"{q}{i}" for q in DMAQ for i in range(NSEMQ)]:
        em.sem_handles[s] = ctx.enter_context(nc.semaphore(f"sem_{s}"))

    RD = lambda *k: tuple(k)

    def split3(x_ap, n, o_hi, o_mid, o_lo, s1_ap, s2_ap, rkey, wkeys, skeys, eng="dve"):
        hi_k, mid_k, lo_k = wkeys
        s1k, s2k = skeys
        em.op(eng, lambda E: E.tensor_copy(o_hi, x_ap), RD(rkey), RD(hi_k))
        em.op(eng, lambda E: E.tensor_copy(s1_ap, o_hi), RD(hi_k), RD(s1k))
        em.op(eng, lambda E: E.tensor_sub(s2_ap, x_ap, s1_ap), RD(rkey, s1k), RD(s2k))
        em.op(eng, lambda E: E.tensor_copy(o_mid, s2_ap), RD(s2k), RD(mid_k))
        em.op(eng, lambda E: E.tensor_copy(s1_ap, o_mid), RD(mid_k), RD(s1k))
        em.op(eng, lambda E: E.tensor_sub(s2_ap, s2_ap, s1_ap), RD(s2k, s1k), RD(s2k))
        em.op(eng, lambda E: E.tensor_copy(o_lo, s2_ap), RD(s2k), RD(lo_k))

    em.dma("qg", lambda E: E.dma_start(out=idb[:], in_=idb_h[:]), (), RD("idb"))
    em.dma("qg", lambda E: E.dma_start(out=b1a[:], in_=b1a_h[:]), (), RD("b1a"))
    em.dma("qg", lambda E: E.dma_start(out=b1g[:], in_=b1g_h[:]), (), RD("b1g"))
    em.dma("qg", lambda E: E.dma_start(out=b2b[:], in_=b2b_h[:]), (), RD("b2b"))
    em.dma("qg", lambda E: E.dma_start(out=cwc[:], in_=cw_h[:]), (), RD("cwc"))
    em.dma("qg", lambda E: E.dma_start(out=cbc[:], in_=cb_h[:]), (), RD("cbc"))
    em.dma("qg", lambda E: E.dma_start(out=fsc[:, 0:8 * 15].rearrange("p (a b) -> p a b", b=15),
                                       in_=csw_h[:].rearrange("p (a b) -> p a b", b=15)), (), RD("fsc0"))
    em.op("gp", lambda E: E.memset(onesb[:], 1.0), (), RD("onesb"))
    em.op("gp", lambda E: E.memset(Srow[:], 0.0), (), RD("Srow_z"))
    em.op("gp", lambda E: E.memset(colk[:, 0:1], 0.5), (), RD("halfcol"))
    em.op("dve", lambda E: E.memset(rsp[:], 0.0), (), RD("rsp"))
    em.op("dve", lambda E: E.memset(rspt[:], 0.0), (), RD("rsp"))

    csw3 = fsc[:, 0:120].rearrange("p (a b) -> p a b", b=15)
    fmx = fsc[:, 120:128].rearrange("p (a b) -> p a b", b=1)
    em.op("dve", lambda E: E.reduce_max(fmx, csw3, AX.X), RD("fsc0"), RD("fmx"))
    filt3 = filt[:].rearrange("p (a b) -> p a b", b=15)
    em.op("dve", lambda E: E.tensor_sub(filt3, csw3, fmx.to_broadcast([128, 8, 15])),
          RD("fsc0", "fmx"), RD("filt"))
    em.op("act", lambda E: E.activation(filt[:], filt[:], Act.Exp), RD("filt"), RD("filt"))
    em.op("dve", lambda E: E.reduce_sum(fmx, filt3, AX.X), RD("filt"), RD("fmx"))
    em.op("dve", lambda E: E.reciprocal(fsc[:, 120:128], fsc[:, 120:128]), RD("fmx"), RD("fmx"))
    em.op("dve", lambda E: E.tensor_mul(filt3, filt3, fmx.to_broadcast([128, 8, 15])),
          RD("filt", "fmx"), RD("filt"))
    for j in range(KC):
        for k in range(15):
            em.op("dve", lambda E, j=j, k=k: E.tensor_scalar_mul(
                convT[:, k * 128:(k + 1) * 128], idb[:], filt[:, j * 15 + k:j * 15 + k + 1]),
                RD("idb", "filt"), RD("convT0", "convT1", "convT2", "convT3"))
        em.dma("qs", lambda E, j=j: E.dma_start(out=scr_dg[j], in_=convT[:, 0:15 * 128]),
               RD("convT0", "convT1", "convT2", "convT3"), RD(f"scrdg_{j}"))

    for dc in range(KC):
        em.dma("qg", lambda E, dc=dc: E.dma_start(out=qldA[:], in_=W2_h[dc * 128:(dc + 1) * 128, :]),
               (), RD("oacc0"))
        em.op("gp", lambda E, dc=dc: E.tensor_copy(qw[:, 0:D], qldA[:]),
              RD("oacc0"), RD("convT0", "convT1"))
        em.dma("qg", lambda E, dc=dc: E.dma_start(out=scr_W2[dc], in_=qw[:, 0:D]),
               RD("convT0", "convT1"), RD(f"scrW2_{dc}"))

    def colsum(cols, outcol, tag):
        em.op("dve", lambda E, a=cols[0], b=cols[1], o=outcol:
              E.tensor_add(qabs[:, o:o + 1], qabs[:, a:a + 1], qabs[:, b:b + 1]),
              RD(f"qab{cols[0]}", f"qab{cols[1]}"), RD(f"qab{outcol}"))
        for c in cols[2:]:
            em.op("dve", lambda E, c=c, o=outcol:
                  E.tensor_add(qabs[:, o:o + 1], qabs[:, o:o + 1], qabs[:, c:c + 1]),
                  RD(f"qab{c}", f"qab{outcol}"), RD(f"qab{outcol}"))

    def psum_partition_sum(colidx, outcol, scale_mul, tag):
        ck = f"qab{colidx}"
        split3(qabs[:, colidx:colidx + 1], 1,
               rsp[:, 0:1], rsp[:, 1:2], rsp[:, 2:3], rspt[:, 0:1], rspt[:, 1:2],
               ck, ("rsp", "rsp", "rsp"), ("rsp", "rsp"))
        for k in range(3):
            em.op("pe", lambda E, k=k: E.matmul(pb[3][0:1, 0:1], rsp[:, k:k + 1], onesb[:, 0:1],
                                                start=(k == 0), stop=(k == 2)),
                  RD("rsp", "onesb"), RD("pb3"))
        em.op("dve", lambda E: E.tensor_scalar(st1[0:1, 0:1], pb[3][0:1, 0:1], scale_mul, 1e-5,
                                               Alu.mult, Alu.max),
              RD("pb3"), RD(f"sc{tag}"))
        split3(st1[0:1, 0:1], 1, rsp[0:1, 3:4], rsp[0:1, 4:5], rsp[0:1, 5:6],
               rspt[0:1, 2:3], rspt[0:1, 3:4],
               f"sc{tag}", ("rsp", "rsp", "rsp"), ("rsp", "rsp"))
        for k in range(3):
            em.op("pe", lambda E, k=k: E.matmul(pb[3][:, 8:9], onesb[:], rsp[:, 3 + k:4 + k],
                                                start=(k == 0), stop=(k == 2)),
                  RD("rsp", "onesb"), RD("pb3"))
        em.op("dve", lambda E: E.tensor_copy(qabs[:, outcol:outcol + 1], pb[3][:, 8:9]),
              RD("pb3"), RD(f"qab{outcol}"))

    def quant_chunk(buf_ap, bk, scale_col, scale_key, out_bf_ap, out_keys, width):
        em.op("dve", lambda E: E.tensor_scalar_mul(qv[:, 0:width], buf_ap, scale_col),
              RD(bk, scale_key), RD("oacc2"))
        em.op("dve", lambda E: E.tensor_single_scalar(qc[:, 0:width], qv[:, 0:width], 0.5, Alu.is_ge),
              RD("oacc2"), RD("oacc3"))
        em.op("dve", lambda E: E.tensor_single_scalar(qv[:, 0:width], qv[:, 0:width], -0.5, Alu.is_le),
              RD("oacc2"), RD("oacc2"))
        em.op("dve", lambda E: E.tensor_sub(out_bf_ap, qc[:, 0:width], qv[:, 0:width]),
              RD("oacc3", "oacc2"), out_keys)

    startup_units = []

    def _mu_w1(i):
        buf = qldA if i % 2 == 0 else qldB
        bk = "oacc0" if i % 2 == 0 else "oacc1"
        kc, qq = i // 4, i % 4
        em.dma(("qg", "qs", "qa")[i % 3], lambda E, kc=kc, qq=qq, buf=buf: E.dma_start(
            out=buf[:], in_=w1T_h[kc * 128:(kc + 1) * 128, qq * D:(qq + 1) * D]), (), RD(bk))
        em.op("act", lambda E, buf=buf, i=i: E.activation(qc[:], buf[:], Act.Abs,
                                                          accum_out=qabs[:, i % 32:i % 32 + 1]),
              RD(bk), RD("oacc3", f"qab{i % 32}"))

    def _mu_w2(i):
        buf = qldA if i % 2 == 0 else qldB
        bk = "oacc0" if i % 2 == 0 else "oacc1"
        em.dma(("qa", "qg", "qs")[i % 3], lambda E, i=i, buf=buf: E.dma_start(
            out=buf[:], in_=w2T_h[i * 128:(i + 1) * 128, :]), (), RD(bk))
        em.op("act", lambda E, buf=buf: E.activation(qc[:], buf[:], Act.Abs,
                                                     accum_out=qabs[:, 36:37]),
              RD(bk), RD("oacc3", "qab36"))
        if i == 0:
            em.op("dve", lambda E: E.tensor_copy(qabs[:, 38:39], qabs[:, 36:37]),
                  RD("qab36"), RD("qab38"))
        else:
            em.op("dve", lambda E: E.tensor_add(qabs[:, 38:39], qabs[:, 38:39], qabs[:, 36:37]),
                  RD("qab36", "qab38"), RD("qab38"))

    def _scales_w1():
        colsum(list(range(32)), 32, "w1")
        psum_partition_sum(32, 33, 1.0 / (HID * D), "m1")
        em.op("dve", lambda E: E.reciprocal(qabs[:, 34:35], qabs[:, 33:34]), RD("qab33"), RD("qab34"))
        em.op("dve", lambda E: E.tensor_scalar_mul(cwc[:], cwc[:], qabs[:, 33:34]),
              RD("cwc", "qab33"), RD("cwc"))

    def _scales_w2():
        psum_partition_sum(38, 35, 1.0 / (D * HID), "m2")
        em.op("dve", lambda E: E.reciprocal(qabs[:, 39:40], qabs[:, 35:36]), RD("qab35"), RD("qab39"))

    def _qu_w1(hc):
        buf = qldA if hc % 2 == 0 else qldB
        bk = "oacc0" if hc % 2 == 0 else "oacc1"
        w1v = w1T_h.rearrange("(kc p) h -> kc p h", p=128)
        q = ("qs", "qa", "qg")[hc % 3]
        em.dma(q, lambda E, hc=hc, buf=buf: E.dma_start(
            out=buf[:].rearrange("p (kc c) -> p kc c", c=128),
            in_=w1v[:, :, hc * 128:(hc + 1) * 128].rearrange("kc p c -> p kc c")),
            (), RD(bk))
        quant_chunk(buf[:], bk, qabs[:, 34:35], "qab34", qw[:, 0:D], ("convT0", "convT1"), D)
        em.dma(q, lambda E, hc=hc: E.dma_start(out=scr_q1[hc], in_=qw[:, 0:D]),
               RD("convT0", "convT1"), RD(f"scrq1_{hc}"))

    def _qu_w2(hc):
        buf = qldA if hc % 2 == 0 else qldB
        bk = "oacc0" if hc % 2 == 0 else "oacc1"
        q = ("qa", "qg", "qs")[hc % 3]
        em.dma(q, lambda E, hc=hc, buf=buf: E.dma_start(
            out=buf[:], in_=w2T_h[hc * 128:(hc + 1) * 128, :]), (), RD(bk))
        quant_chunk(buf[:], bk, qabs[:, 39:40], "qab39", qw[:, 0:D], ("convT0", "convT1"), D)
        em.dma(q, lambda E, hc=hc: E.dma_start(out=scr_q2[hc], in_=qw[:, 0:D]),
               RD("convT0", "convT1"), RD(f"scrq2_{hc}"))

    for i in range(32):
        startup_units.append(lambda i=i: _mu_w1(i))
    for i in range(32):
        startup_units.append(lambda i=i: _mu_w2(i))
    startup_units.append(lambda: _scales_w1())
    startup_units.append(lambda: _scales_w2())
    startup_units.extend(lambda hc=hc: _qu_w1(hc) for hc in range(HCN))
    startup_units.extend(lambda hc=hc: _qu_w2(hc) for hc in range(HCN))

    def drain(lst, n):
        for _ in range(min(n, len(lst))):
            lst.pop(0)()

    W1v = W1_h.rearrange("(kc p) e -> kc p e", p=128)

    def _w1prep(half, ec):
        buf = qldA if ec % 2 == 0 else qldB
        bk = "oacc0" if ec % 2 == 0 else "oacc1"
        slab = slW1[:, half * KC * 128:(half + 1) * KC * 128]
        slk = f"slW1_{half}"
        em.dma("qs", lambda E, ec=ec, buf=buf: E.dma_start(
            out=buf[:].rearrange("p (kc c) -> p kc c", c=128),
            in_=W1v[:, :, ec * 128:(ec + 1) * 128].rearrange("kc p c -> p kc c")),
            (), RD(bk))
        em.op("gp", lambda E, buf=buf, slab=slab: E.tensor_copy(slab, buf[:]),
              RD(bk), RD(slk))
        em.dma("qs", lambda E, ec=ec, slab=slab: E.dma_start(out=scr_W1[ec], in_=slab),
               RD(slk), RD(f"scrW1_{ec}"))

    w1prep_units = [(half, ec) for j0 in range(KC) for half, ec in ((0, 8 + j0), (1, j0))]

    def producer(p):
        xr = xring[:, (p % 2) * D:(p % 2 + 1) * D]
        xrk = f"xr{p % 2}"
        xn = xnring[:, (p % 2) * D:(p % 2 + 1) * D]
        xnk = f"xn{p % 2}"
        xb = xnbf[:, (p % 2) * D:(p % 2 + 1) * D]
        xbk = f"xb{p % 2}"
        nb = nbf[:, (p % 2) * D:(p % 2 + 1) * D]
        nbk = f"nb{p % 2}"
        em.dma("qs", lambda E, p=p: E.dma_start(out=xr, in_=x_h[p * 128:(p + 1) * 128, :]),
               (), RD(xrk))
        em.op("dve", lambda E: E.reduce_sum(colk[:, 2:3], xr, AX.X), RD(xrk), RD("pc2"))
        em.op("act", lambda E: E.activation(pt2[:], xr, Act.Square, accum_out=colk[:, 3:4]),
              RD(xrk), RD("pt2", "pc3"))
        em.op("dve", lambda E: E.tensor_scalar_mul(colk[:, 2:3], colk[:, 2:3], 1.0 / D),
              RD("pc2"), RD("pc2"))
        em.op("dve", lambda E: E.tensor_scalar_mul(colk[:, 3:4], colk[:, 3:4], 1.0 / D),
              RD("pc3"), RD("pc3"))
        em.op("dve", lambda E: E.tensor_mul(colk[:, 4:5], colk[:, 2:3], colk[:, 2:3]),
              RD("pc2"), RD("pc4"))
        em.op("dve", lambda E: E.tensor_sub(colk[:, 4:5], colk[:, 3:4], colk[:, 4:5]),
              RD("pc3", "pc4"), RD("pc4"))
        em.op("dve", lambda E: E.tensor_scalar_add(colk[:, 4:5], colk[:, 4:5], 1e-5),
              RD("pc4"), RD("pc4"))
        em.op("act", lambda E: E.activation(colk[:, 5:6], colk[:, 4:5], Act.Sqrt),
              RD("pc4"), RD("pc5"))
        em.op("dve", lambda E: E.reciprocal(colk[:, 5:6], colk[:, 5:6]), RD("pc5"), RD("pc5"))
        for _ in range(2):
            em.op("dve", lambda E: E.tensor_mul(colk[:, 6:7], colk[:, 5:6], colk[:, 5:6]),
                  RD("pc5"), RD("pc6"))
            em.op("dve", lambda E: E.tensor_mul(colk[:, 6:7], colk[:, 6:7], colk[:, 4:5]),
                  RD("pc6", "pc4"), RD("pc6"))
            em.op("dve", lambda E: E.tensor_scalar(colk[:, 6:7], colk[:, 6:7], -0.5, 1.5,
                                                   Alu.mult, Alu.add), RD("pc6"), RD("pc6"))
            em.op("dve", lambda E: E.tensor_mul(colk[:, 5:6], colk[:, 5:6], colk[:, 6:7]),
                  RD("pc5", "pc6"), RD("pc5"))
        em.op("dve", lambda E: E.tensor_scalar(xn, xr, colk[:, 2:3], colk[:, 5:6],
                                               Alu.subtract, Alu.mult),
              RD(xrk, "pc2", "pc5"), RD(xnk))
        em.op("dve", lambda E: E.tensor_copy(xb, xn), RD(xnk), RD(xbk))
        em.op("dve", lambda E: E.reduce_max(colk[:, 7:8], xn, AX.X, apply_absolute_value=True),
              RD(xnk), RD("pc7"))
        em.op("dve", lambda E: E.tensor_single_scalar(colk[:, 7:8], colk[:, 7:8], 1e-5, Alu.max),
              RD("pc7"), RD("pc7"))
        em.op("dve", lambda E: E.reciprocal(colk[:, 8:9], colk[:, 7:8]), RD("pc7"), RD("pc8"))
        em.op("dve", lambda E: E.tensor_scalar_mul(colk[:, 8:9], colk[:, 8:9], 127.0),
              RD("pc8"), RD("pc8"))
        em.op("dve", lambda E: E.tensor_scalar_mul(colk[:, 9:10], colk[:, 7:8], -1.0 / 127.0),
              RD("pc7"), RD("pc9"))
        em.op("act", lambda E: E.activation(pt1[:], xn, Act.Copy, bias=0.5, scale=colk[:, 8:9]),
              RD(xnk, "pc8"), RD("pt1"))
        em.op("act", lambda E: E.activation(pti[:], pt1[:], Act.Copy), RD("pt1"), RD("pti"))
        em.op("dve", lambda E: E.tensor_sub(pt2[:], pt1[:], pti[:]), RD("pt1", "pti"), RD("pt2"))
        em.op("dve", lambda E: E.scalar_tensor_tensor(pt1[:], pt2[:], 1.0, pti[:], Alu.is_ge, Alu.add),
              RD("pt2", "pti"), RD("pt1"))
        em.op("dve", lambda E: E.scalar_tensor_tensor(pt2[:], pt2[:], 0.0, pt1[:], Alu.is_lt, Alu.subtract),
              RD("pt2", "pt1"), RD("pt2"))
        em.op("dve", lambda E: E.tensor_scalar_mul(nb, pt2[:], colk[:, 9:10]),
              RD("pt2", "pc9"), RD(nbk))
        g_of = (p - 1) // 4
        j = (p - 1) % 4
        if g_of < NG:
            par = g_of % 2
            for kc in range(KC):
                dst = xnT[:, par * KC * BUFW + kc * BUFW + 16 + j * 128:
                          par * KC * BUFW + kc * BUFW + 16 + (j + 1) * 128]
                slotA = pbT[:, 768:896]
                em.op("pe", lambda E, slotA=slotA, kc=kc, xb=xb: E.transpose(
                    slotA, xb[:, kc * 128:(kc + 1) * 128], idb[:]),
                    RD(xbk, "idb", "pbT"), RD("pbT"), tr=True)
                em.op("dve", lambda E, dst=dst, slotA=slotA: E.tensor_copy(dst, slotA),
                      RD("pbT"), RD(f"xnT{par}", "pbT"))
                dst2 = nT[:, par * KC * BUFW + kc * BUFW + 16 + j * 128:
                          par * KC * BUFW + kc * BUFW + 16 + (j + 1) * 128]
                slotB = pbT[:, 896:1024]
                em.op("pe", lambda E, slotB=slotB, kc=kc, nb=nb: E.transpose(
                    slotB, nb[:, kc * 128:(kc + 1) * 128], idb[:]),
                    RD(nbk, "idb", "pbT"), RD("pbT"), tr=True)
                em.op("dve", lambda E, dst2=dst2, slotB=slotB: E.tensor_copy(dst2, slotB),
                      RD("pbT"), RD(f"nT{par}", "pbT"))
        if j == 0 and g_of >= 1 and g_of - 1 < NG:
            parh = (g_of - 1) % 2
            par0 = g_of % 2
            if g_of < NG:
                xv = xnT[:].rearrange("p (q kc c) -> p q kc c", q=2, c=BUFW)
                nv = nT[:].rearrange("p (q kc c) -> p q kc c", q=2, c=BUFW)
                em.op("dve", lambda E, xv=xv: E.tensor_copy(
                    xv[:, parh, :, 528:544], xv[:, par0, :, 16:32]),
                    RD(f"xnT{par0}"), RD(f"xnT{parh}"))
                em.op("dve", lambda E, nv=nv: E.tensor_copy(
                    nv[:, parh, :, 528:544], nv[:, par0, :, 16:32]),
                    RD(f"nT{par0}"), RD(f"nT{parh}"))
            else:
                for kc in range(KC):
                    dsth = xnT[:, parh * KC * BUFW + kc * BUFW + 528:
                               parh * KC * BUFW + kc * BUFW + 544]
                    slotA = pbT[:, 768:896]
                    em.op("pe", lambda E, slotA=slotA, kc=kc, xb=xb: E.transpose(
                        slotA, xb[:, kc * 128:(kc + 1) * 128], idb[:]),
                        RD(xbk, "idb", "pbT"), RD("pbT"), tr=True)
                    em.op("dve", lambda E, dsth=dsth, slotA=slotA: E.tensor_copy(
                        dsth, slotA[:, 0:16]), RD("pbT"), RD(f"xnT{parh}", "pbT"))
                    dsth2 = nT[:, parh * KC * BUFW + kc * BUFW + 528:
                               parh * KC * BUFW + kc * BUFW + 544]
                    slotB = pbT[:, 896:1024]
                    em.op("pe", lambda E, slotB=slotB, kc=kc, nb=nb: E.transpose(
                        slotB, nb[:, kc * 128:(kc + 1) * 128], idb[:]),
                        RD(nbk, "idb", "pbT"), RD("pbT"), tr=True)
                    em.op("dve", lambda E, dsth2=dsth2, slotB=slotB: E.tensor_copy(
                        dsth2, slotB[:, 0:16]), RD("pbT"), RD(f"nT{parh}", "pbT"))

    def halo_left(g):
        par = g % 2
        if g == 0:
            for kc in range(KC):
                em.op("dve", lambda E, kc=kc: E.memset(
                    xnT[:, kc * BUFW:kc * BUFW + 16], 0.0), (), RD("xnT0"))
                em.op("dve", lambda E, kc=kc: E.memset(
                    nT[:, kc * BUFW:kc * BUFW + 16], 0.0), (), RD("nT0"))
            return
        spar = (g - 1) % 2
        for kc in range(KC):
            em.op("dve", lambda E, kc=kc: E.tensor_copy(
                xnT[:, par * KC * BUFW + kc * BUFW:par * KC * BUFW + kc * BUFW + 16],
                xnT[:, spar * KC * BUFW + kc * BUFW + 512:spar * KC * BUFW + kc * BUFW + 528]),
                RD(f"xnT{spar}"), RD(f"xnT{par}"))
            em.op("dve", lambda E, kc=kc: E.tensor_copy(
                nT[:, par * KC * BUFW + kc * BUFW:par * KC * BUFW + kc * BUFW + 16],
                nT[:, spar * KC * BUFW + kc * BUFW + 512:spar * KC * BUFW + kc * BUFW + 528]),
                RD(f"nT{spar}"), RD(f"nT{par}"))

    def attn_W1(g, cb=None):
        par = g % 2
        xnTk = f"xnT{par}"
        base = par * KC * BUFW
        for j in range(KC):
            for half, ec in ((0, 8 + j), (1, j)):
                bank = pb[half]
                bkey = f"pb{half}"
                slab = slW1[:, half * KC * 128:(half + 1) * KC * 128]
                slk = f"slW1_{half}"
                if g > 0:
                    em.dma("qs", lambda E, ec=ec, slab=slab: E.dma_start(out=slab, in_=scr_W1[ec]),
                           RD(f"scrW1_{ec}"), RD(slk))
                else:
                    _w1prep(*w1prep_units.pop(0))
                bslot = pb[2][:, (ec % 16) * 32:(ec % 16) * 32 + 16]
                bskey = "pb2"
                for kc in range(KC):
                    lhs = slab[:, kc * 128:(kc + 1) * 128]
                    rhsA = xnT[:, base + kc * BUFW + 8:base + kc * BUFW + 520]
                    rhsB = xnT[:, base + kc * BUFW + 520:base + kc * BUFW + 536]
                    em.op("pe", lambda E, lhs=lhs, rhsA=rhsA, bank=bank, kc=kc:
                          E.matmul(bank[:, 0:512], lhs, rhsA, start=(kc == 0), stop=(kc == KC - 1)),
                          RD(slk, xnTk), RD(bkey))
                    em.op("pe", lambda E, lhs=lhs, rhsB=rhsB, bslot=bslot, kc=kc:
                          E.matmul(bslot, lhs, rhsB, start=(kc == 0), stop=(kc == KC - 1)),
                          RD(slk, xnTk), RD(bskey))
            gb = pb[0]; gbk = "pb0"
            ab = pb[1]; abk = "pb1"
            gbs = pb[2][:, ((8 + j) % 16) * 32:((8 + j) % 16) * 32 + 16]; gbsk = "pb2"
            abs_ = pb[2][:, (j % 16) * 32:(j % 16) * 32 + 16]; absk = "pb2"
            sgj = sg[:, (j % 2) * AWIN:(j % 2 + 1) * AWIN]
            sgk = f"sg{j % 2}"
            xgj = xglu[:, j * AWIN:(j + 1) * AWIN]
            xgk = f"xglu{j}"
            em.op("act", lambda E, sgj=sgj, gb=gb, j=j: E.activation(
                sgj[:, 0:512], gb[:, 0:512], Act.Sigmoid, bias=b1g[:, j:j + 1]),
                RD(gbk, "b1g"), RD(sgk))
            em.op("act", lambda E, sgj=sgj, gbs=gbs, j=j: E.activation(
                sgj[:, 512:528], gbs, Act.Sigmoid, bias=b1g[:, j:j + 1]),
                RD(gbsk, "b1g"), RD(sgk))
            em.op("dve", lambda E, xgj=xgj, ab=ab, sgj=sgj, j=j: E.scalar_tensor_tensor(
                xgj[:, 0:512], ab[:, 0:512], b1a[:, j:j + 1], sgj[:, 0:512], Alu.add, Alu.mult),
                RD(abk, "b1a", sgk), RD(xgk))
            em.op("dve", lambda E, xgj=xgj, abs_=abs_, sgj=sgj, j=j: E.scalar_tensor_tensor(
                xgj[:, 512:528], abs_, b1a[:, j:j + 1], sgj[:, 512:528], Alu.add, Alu.mult),
                RD(absk, "b1a", sgk), RD(xgk))
            if g == 0:
                em.op("dve", lambda E, xgj=xgj: E.memset(xgj[:, 0:8], 0.0), (), RD(xgk))
            if cb is not None:
                cb(j)

    def attn_rest(g):
        for j in range(KC):
            xgj = xglu[:, j * AWIN:(j + 1) * AWIN]
            xgk = f"xglu{j}"
            dj = dsc[:, (j % 2) * 15 * 128:(j % 2 + 1) * 15 * 128]
            djk = f"dsc{j % 2}"
            em.dma("qg", lambda E, j=j, dj=dj: E.dma_start(out=dj, in_=scr_dg[j]),
                   RD(f"scrdg_{j}"), RD(djk))
            cbank = pb[j % 2]
            cbk = f"pb{j % 2}"
            for k in range(15):
                em.op("pe", lambda E, dj=dj, xgj=xgj, k=k, cbank=cbank: E.matmul(
                    cbank[:, 0:512], dj[:, k * 128:(k + 1) * 128], xgj[:, k + 1:k + 1 + TG],
                    start=(k == 0), stop=(k == 14)), RD(djk, xgk), RD(cbk))
            cT = convT[:, j * TG:(j + 1) * TG]
            em.op("dve", lambda E, cT=cT, cbank=cbank: E.tensor_copy(cT, cbank[:, 0:512]),
                  RD(cbk), RD(f"convT{j}"))
        for pair in range(2):
            for stj2 in range(2):
                stj = 2 * pair + stj2
                xo = xout[:, 0:D]
                xok = "xout0"
                u = PADL + g * TG + stj * 128
                em.dma("qs", lambda E, u=u, xo=xo, stj=stj: E.dma_start(out=xo, in_=x_h[u:u + 128, :]),
                       (), RD(xok)) if stj2 == 99 else None
            for kc in range(KC):
                slab = slW2[:, (kc % 2) * D:(kc % 2 + 1) * D]
                slk = f"slW2_{kc % 2}"
                em.dma("qg", lambda E, kc=kc, slab=slab: E.dma_start(out=slab, in_=scr_W2[kc]),
                       RD(f"scrW2_{kc}"), RD(slk))
                for stj2 in range(2):
                    stj = 2 * pair + stj2
                    bk0 = pb[0] if stj2 == 0 else pb[2]
                    bk1 = pb[1] if stj2 == 0 else pb[6]
                    b0key = "pb0" if stj2 == 0 else "pb2"
                    b1key = "pb1" if stj2 == 0 else "pb6"
                    lhs = convT[:, kc * TG + stj * 128:kc * TG + (stj + 1) * 128]
                    em.op("pe", lambda E, lhs=lhs, bk0=bk0, slab=slab, kc=kc: E.matmul(
                        bk0[:, 0:512], lhs, slab[:, 0:512],
                        start=(kc == 0), stop=(kc == KC - 1)),
                        RD(f"convT{kc}", slk), RD(b0key))
                    em.op("pe", lambda E, lhs=lhs, bk1=bk1, slab=slab, kc=kc: E.matmul(
                        bk1[:, 0:512], lhs, slab[:, 512:1024],
                        start=(kc == 0), stop=(kc == KC - 1)),
                        RD(f"convT{kc}", slk), RD(b1key))
            for stj2 in range(2):
                stj = 2 * pair + stj2
                xo = xout[:, 0:D]
                xok = "xout0"
                u = PADL + g * TG + stj * 128
                em.dma("qs", lambda E, u=u, xo=xo: E.dma_start(out=xo, in_=x_h[u:u + 128, :]),
                       (), RD(xok))
                em.op("gp", lambda E, xo=xo: E.tensor_add(xo, xo, b2b[:]), RD(xok, "b2b"), RD(xok))
                bk0 = pb[0] if stj2 == 0 else pb[2]
                bk1 = pb[1] if stj2 == 0 else pb[6]
                b0key = "pb0" if stj2 == 0 else "pb2"
                b1key = "pb1" if stj2 == 0 else "pb6"
                oa = out_acc[:, stj * D:(stj + 1) * D]
                oak = f"oacc{stj}"
                em.op("dve", lambda E, oa=oa, bk0=bk0, xo=xo: E.tensor_add(oa[:, 0:512], bk0[:, 0:512],
                                                                           xo[:, 0:512]),
                      RD(b0key, xok), RD(oak))
                em.op("dve", lambda E, oa=oa, bk1=bk1, xo=xo: E.tensor_add(oa[:, 512:1024], bk1[:, 0:512],
                                                                           xo[:, 512:1024]),
                      RD(b1key, xok), RD(oak))
                if dbg_h is not None and g == 0:
                    em.dma("qs", lambda E, stj=stj, oa=oa: E.dma_start(
                        out=dbg_h[stj * 128:(stj + 1) * 128, :], in_=oa), RD(oak), RD(f"dbg{stj}"))

    def ffn1_half(g, hh, ucb=None):
        par = g % 2
        nTk = f"nT{par}"
        base = par * KC * BUFW
        woff = 16 + hh * HALF - 2
        for hc in range(HCN):
            if ucb is not None:
                ucb(hc)
            slab = slq1[:, (hc % 2) * KC * 128:(hc % 2 + 1) * KC * 128]
            slk = f"slq1_{hc % 2}"
            em.dma("qg", lambda E, hc=hc, slab=slab: E.dma_start(out=slab, in_=scr_q1[hc]),
                   RD(f"scrq1_{hc}"), RD(slk))
            bank = pb[3 + hc % 3]
            bkey = f"pb{3 + hc % 3}"
            for kc in range(KC):
                lhs = slab[:, kc * 128:(kc + 1) * 128]
                rhs = nT[:, base + kc * BUFW + woff:base + kc * BUFW + woff + FWIN]
                em.op("pe", lambda E, lhs=lhs, rhs=rhs, bank=bank, kc=kc: E.matmul(
                    bank[:, 0:FWIN], lhs, rhs, start=(kc == 0), stop=(kc == KC - 1)),
                    RD(slk, nTk), RD(bkey))
            Aap = cvA[:, (hc % 2) * HALF:(hc % 2 + 1) * HALF]
            Ak = f"cvA{hc % 2}"
            Bap = cvB[:, (hc % 2) * HALF:(hc % 2 + 1) * HALF]
            Bk = f"cvB{hc % 2}"
            Yap = cvY[:, (hc % 2) * HALF:(hc % 2 + 1) * HALF]
            Yk = f"cvY{hc % 2}"
            w0 = cwc[:, hc * 3 + 0:hc * 3 + 1]
            w1c = cwc[:, hc * 3 + 1:hc * 3 + 2]
            w2c = cwc[:, hc * 3 + 2:hc * 3 + 3]
            cb = cbc[:, hc:hc + 1]
            em.op("dve", lambda E, Aap=Aap, bank=bank, w1c=w1c, cb=cb: E.tensor_scalar(
                Aap, bank[:, 2:2 + HALF], w1c, cb, Alu.mult, Alu.add),
                RD(bkey, "cwc", "cbc"), RD(Ak))
            em.op("dve", lambda E, Bap=Bap, bank=bank, w0=w0, Aap=Aap: E.scalar_tensor_tensor(
                Bap, bank[:, 1:1 + HALF], w0, Aap, Alu.mult, Alu.add),
                RD(bkey, "cwc", Ak), RD(Bk))
            em.op("dve", lambda E, Yap=Yap, bank=bank, w2c=w2c, Bap=Bap: E.scalar_tensor_tensor(
                Yap, bank[:, 3:3 + HALF], w2c, Bap, Alu.mult, Alu.add),
                RD(bkey, "cwc", Bk), RD(Yk))
            h2ap = h2[:, hc * HALF:(hc + 1) * HALF]
            em.op("act", lambda E, h2ap=h2ap, Yap=Yap: E.activation(h2ap, Yap, Act.Silu),
                  RD(Yk), RD(f"h2_{hc}"))
            if hc % 4 == 3:
                q4 = hc - 3
                h2b = h2[:, q4 * HALF:(q4 + 4) * HALF]
                h2keys4 = tuple(f"h2_{i}" for i in range(q4, q4 + 4))
                for op, store, storek in ((Alu.max, Mhi, "Mhi"), (Alu.min, sB, "sB")):
                    em.op("dve", lambda E, op=op, h2b=h2b: E.tensor_tensor(
                        Mbig[:, 0:512], h2b[:, 0:512], h2b[:, 512:1024], op), h2keys4, RD("Mbig"))
                    em.op("dve", lambda E, op=op: E.tensor_tensor(
                        Mbig[:, 512:768], Mbig[:, 0:256], Mbig[:, 256:512], op), RD("Mbig"), RD("Mbig"))
                    if hc == 3:
                        em.op("dve", lambda E, store=store: E.tensor_copy(store[:], Mbig[:, 512:768]),
                              RD("Mbig"), RD(storek))
                    else:
                        em.op("dve", lambda E, op=op, store=store: E.tensor_tensor(
                            store[:], store[:], Mbig[:, 512:768], op), RD("Mbig", storek), RD(storek))

    def ffn2_half(g, hh, ucb=None):
        par = g % 2
        em.op("dve", lambda E: E.tensor_scalar_mul(sB[:], sB[:], -1.0), RD("sB"), RD("sB"))
        em.op("dve", lambda E: E.tensor_max(sB[:], sB[:], Mhi[:]), RD("sB", "Mhi"), RD("sB"))
        split3(sB[:], HALF, shi[:], smid[:], slo[:], st1[:], st2[:],
               "sB", ("shi", "smid", "slo"), ("st1", "st2"))
        for blk in range(2):
            for ki, (sp, spk) in enumerate(((shi, "shi"), (smid, "smid"), (slo, "slo"))):
                em.op("pe", lambda E, sp=sp, blk=blk, ki=ki: E.transpose(
                    pbT[:, ki * 128:(ki + 1) * 128], sp[:, blk * 128:(blk + 1) * 128], idb[:]),
                    RD(spk, "idb"), RD("pbT"), tr=True)
            em.op("dve", lambda E: E.tensor_copy(rc[:], pbT[:, 0:128]), RD("pbT"), RD("rc"))
            em.op("dve", lambda E: E.tensor_add(rc[:], rc[:], pbT[:, 128:256]), RD("pbT", "rc"), RD("rc"))
            em.op("dve", lambda E: E.tensor_add(rc[:], rc[:], pbT[:, 256:384]), RD("pbT", "rc"), RD("rc"))
            em.op("dve", lambda E, blk=blk: E.reduce_max(scol[:, blk:blk + 1], rc[:], AX.X,
                                                         apply_absolute_value=True),
                  RD("rc"), RD(f"scol{blk}"))
        for blk in range(2):
            em.op("dve", lambda E, blk=blk: E.tensor_single_scalar(
                scol[:, blk:blk + 1], scol[:, blk:blk + 1], 1e-5, Alu.max),
                RD(f"scol{blk}"), RD(f"scol{blk}"))
            em.op("dve", lambda E, blk=blk: E.tensor_scalar_mul(
                scol[:, 2 + blk:3 + blk], scol[:, blk:blk + 1], -1.0 / 127.0),
                RD(f"scol{blk}"), RD(f"ecol{blk}"))
            em.op("dve", lambda E, blk=blk: E.tensor_mul(
                scol[:, 2 + blk:3 + blk], scol[:, 2 + blk:3 + blk], qabs[:, 35:36]),
                RD(f"ecol{blk}", "qab35"), RD(f"ecol{blk}"))
            em.op("dve", lambda E, blk=blk: E.reciprocal(scol[:, 4 + blk:5 + blk],
                                                         scol[:, blk:blk + 1]),
                  RD(f"scol{blk}"), RD(f"s2c{blk}"))
            em.op("dve", lambda E, blk=blk: E.tensor_scalar_mul(
                scol[:, 4 + blk:5 + blk], scol[:, 4 + blk:5 + blk], 127.0),
                RD(f"s2c{blk}"), RD(f"s2c{blk}"))
        em.op("dve", lambda E: E.tensor_copy(scol[:, 6:8], scol[:, 4:6]), RD("s2c0", "s2c1"), RD("s2cat"))
        split3(scol[:, 6:8], 2, rsp[:, 0:2], rsp[:, 2:4], rsp[:, 4:6], rspt[:, 0:2], rspt[:, 2:4],
               "s2cat", ("rsp", "rsp", "rsp"), ("rsp", "rsp"))
        for ki in range(3):
            for blk in range(2):
                sl = (2 * ki + blk) * 128
                em.op("pe", lambda E, sl=sl, ki=ki, blk=blk: E.transpose(
                    pbT[0:1, sl:sl + 128],
                    rsp[:, 2 * ki + blk:2 * ki + blk + 1], idb[:]),
                    RD("rsp", "idb"), RD("pbT"), tr=True)
        for ki in range(3):
            for blk in range(2):
                sl = (2 * ki + blk) * 128
                em.op("dve", lambda E, sl=sl, ki=ki, blk=blk: E.tensor_copy(
                    Srow[0:1, ki * HALF + blk * 128:ki * HALF + (blk + 1) * 128],
                    pbT[0:1, sl:sl + 128]),
                    RD("pbT", "Srow_z"), RD("SrowV"))
        for ki in range(3):
            em.op("pe", lambda E, ki=ki: E.matmul(pb[6][:, 0:HALF], onesb[:],
                                                  Srow[:, ki * HALF:(ki + 1) * HALF],
                                                  start=(ki == 0), stop=(ki == 2)),
                  RD("SrowV", "Srow_z", "onesb"), RD("pb6"))
        em.op("dve", lambda E: E.tensor_copy(s2b[:], pb[6][:, 0:HALF]), RD("pb6"), RD("s2b"))
        if dbg2_h is not None and g == 0 and hh == 0:
            em.dma("qs", lambda E: E.dma_start(out=dbg2_h[:], in_=h2[:]),
                   tuple(f"h2_{i}" for i in range(HCN)), RD("dbg2"))
            em.dma("qs", lambda E: E.dma_start(out=dbg3_h[:, 0:HALF], in_=s2b[:]),
                   RD("s2b"), RD("dbg3"))
            em.dma("qs", lambda E: E.dma_start(out=dbg3_h[:, HALF:HALF + 4], in_=scol[:, 2:6]),
                   RD("ecol0", "ecol1", "s2c0", "s2c1"), RD("dbg3"))
            em.dma("qs", lambda E: E.dma_start(out=dbg4_h[:, 0:KC * TG].bitcast(BF16), in_=convT[:]),
                   tuple(f"convT{j}" for j in range(KC)), RD("dbg4"))
        for q in range(KC):
            if ucb is not None:
                ucb(q)
            hcs = range(4 * q, 4 * q + 4)
            h2b = h2[:, 4 * q * HALF:(4 * q + 4) * HALF]
            h2keys = tuple(f"h2_{i}" for i in hcs)
            v = ht1[:, 0:4 * HALF]
            vk = "ht1"
            em.op("dve", lambda E, v=v, h2b=h2b: E.tensor_tensor(
                v.rearrange("p (c t) -> p c t", t=HALF),
                h2b.rearrange("p (c t) -> p c t", t=HALF),
                s2b[:, None, :].to_broadcast([128, 4, HALF]), Alu.mult),
                h2keys + ("s2b",), RD(vk))
            em.op("act", lambda E, v=v: E.activation(pti[:], v, Act.Copy, bias=0.5),
                  RD(vk), RD("pti"))
            em.op("dve", lambda E, v=v: E.scalar_tensor_tensor(
                htc[:], v, 0.5, pti[:], Alu.add, Alu.subtract), RD(vk, "pti"), RD("htc"))
            em.op("dve", lambda E: E.scalar_tensor_tensor(
                htd[:], htc[:], 1.0, pti[:], Alu.is_ge, Alu.add), RD("htc", "pti"), RD("htd"))
            n2a = n2T[:, (q % 2) * 4 * HALF:(q % 2 + 1) * 4 * HALF]
            n2k = f"n2T{q % 2}"
            em.op("dve", lambda E, n2a=n2a: E.scalar_tensor_tensor(
                n2a, htc[:], 0.0, htd[:], Alu.is_lt, Alu.subtract), RD("htc", "htd"), RD(n2k))
            for ii, hc in enumerate(hcs):
                slab2 = slq2[:, (hc % 2) * D:(hc % 2 + 1) * D]
                sl2k = f"slq2_{hc % 2}"
                em.dma("qg", lambda E, hc=hc, slab2=slab2: E.dma_start(out=slab2, in_=scr_q2[hc]),
                       RD(f"scrq2_{hc}"), RD(sl2k))
                for stj in range(2):
                    bA = pb[3 + 2 * stj]
                    bAk = f"pb{3 + 2 * stj}"
                    bB = pb[4 + 2 * stj]
                    bBk = f"pb{4 + 2 * stj}"
                    lhs = n2a[:, ii * HALF + stj * 128:ii * HALF + (stj + 1) * 128]
                    em.op("pe", lambda E, lhs=lhs, bA=bA, slab2=slab2, hc=hc: E.matmul(
                        bA[:, 0:512], lhs, slab2[:, 0:512], start=(hc == 0), stop=(hc == HCN - 1)),
                        RD(n2k, sl2k), RD(bAk))
                    em.op("pe", lambda E, lhs=lhs, bB=bB, slab2=slab2, hc=hc: E.matmul(
                        bB[:, 0:512], lhs, slab2[:, 512:1024], start=(hc == 0), stop=(hc == HCN - 1)),
                        RD(n2k, sl2k), RD(bBk))
        for stj in range(2):
            sub = hh * 2 + stj
            oa = out_acc[:, sub * D:(sub + 1) * D]
            oak = f"oacc{sub}"
            ecol = scol[:, 2 + stj:3 + stj]
            eck = f"ecol{stj}"
            bA = pb[3 + 2 * stj]
            bAk = f"pb{3 + 2 * stj}"
            bB = pb[4 + 2 * stj]
            bBk = f"pb{4 + 2 * stj}"
            em.op("dve", lambda E, oa=oa, bA=bA, ecol=ecol: E.scalar_tensor_tensor(
                oa[:, 0:512], bA[:, 0:512], ecol, oa[:, 0:512], Alu.mult, Alu.add),
                RD(bAk, eck, oak), RD(oak))
            em.op("dve", lambda E, oa=oa, bB=bB, ecol=ecol: E.scalar_tensor_tensor(
                oa[:, 512:1024], bB[:, 0:512], ecol, oa[:, 512:1024], Alu.mult, Alu.add),
                RD(bBk, eck, oak), RD(oak))
            em.dma("qs", lambda E, g=g, sub=sub, oa=oa: E.dma_start(
                out=out_h[g * TG + sub * 128:g * TG + (sub + 1) * 128, :], in_=oa),
                RD(oak), RD(f"outst_{g}_{sub}"))

    for p in range(1, min(4, NSUB) + 1):
        producer(p)
    halo_left(0)
    for p in range(5, min(8, NSUB) + 1):
        producer(p)
    attn_W1(0, cb=lambda j: drain(startup_units, 17))
    drain(startup_units, 10 ** 9)
    for g in range(NG):
        attn_rest(g)
        ffn1_half(g, 0)
        ffn2_half(g, 0)
        ffn1_half(g, 1)
        if g + 1 < NG:
            halo_left(g + 1)
            if 4 * g + 9 <= NSUB:
                producer(4 * g + 9)
            attn_W1(g + 1)
            pend = [p for p in range(4 * g + 10, 4 * g + 13) if p <= NSUB]
            ffn2_half(g, 1, ucb=lambda q, pend=pend: (
                producer(pend.pop(0)) if q in (1, 3, 5) and pend else None))
            for p in pend:
                producer(p)
        else:
            ffn2_half(g, 1)

    need = em.barrier_token([f"outst_{g}_{sub}" for g in range(NG) for sub in range(4)])
    for s, v in sorted(need.items()):
        em.ops["gp"].append(("w", s, v))

    run = _mk_closure_runner(em)
    with nc.Block() as block:
        @block.sync
        def _(e):
            run("sq", e)

        @block.vector
        def _(e):
            run("dve", e)

        @block.scalar
        def _(e):
            run("act", e)

        @block.gpsimd
        def _(e):
            run("gp", e)

        @block.tensor
        def _(e):
            run("pe", e)
    ctx.close()
    return nc


def prep_inputs(L0_attn_inputs, L0_attn_W1, L0_attn_b1, L0_attn_W2, L0_attn_b2,
                L0_attn_conv_shared_weight, L0_ffn_w1, L0_ffn_conv_w, L0_ffn_conv_b,
                L0_ffn_w2):
    x = np.asarray(L0_attn_inputs, dtype=np.float32)
    B, S, _ = x.shape
    W1 = np.ascontiguousarray(np.asarray(L0_attn_W1, np.float32))
    b1 = np.asarray(L0_attn_b1, np.float32)
    W2 = np.ascontiguousarray(np.asarray(L0_attn_W2, np.float32))
    b2 = np.asarray(L0_attn_b2, np.float32)
    csw = np.asarray(L0_attn_conv_shared_weight, np.float32)
    fw1 = np.ascontiguousarray(np.asarray(L0_ffn_w1, np.float32))
    fcw = np.asarray(L0_ffn_conv_w, np.float32)
    fcb = np.asarray(L0_ffn_conv_b, np.float32)
    fw2 = np.ascontiguousarray(np.asarray(L0_ffn_w2, np.float32))

    b1a = np.ascontiguousarray(b1[:D].reshape(8, 128).T)
    b1g = np.ascontiguousarray(b1[D:].reshape(8, 128).T)
    b2b = np.ascontiguousarray(np.broadcast_to(b2[None, :], (128, D)))
    ident = np.eye(128, dtype=np.float32).astype(ml_dtypes.bfloat16)

    def csw_rep(flip):
        c = csw[:, ::-1] if flip else csw
        out = np.zeros((128, 8 * 15), np.float32)
        for j in range(8):
            out[0:64, j * 15:(j + 1) * 15] = c[2 * j]
            out[64:128, j * 15:(j + 1) * 15] = c[2 * j + 1]
        return out

    def cw_rep(flip):
        c = fcw[:, 0, ::-1] if flip else fcw[:, 0, :]
        return np.ascontiguousarray(c.reshape(HCN, 128, 3).transpose(1, 0, 2).reshape(128, HCN * 3))

    cb_rs = np.ascontiguousarray(fcb.reshape(HCN, 128).T)
    fw1t = np.ascontiguousarray(fw1.T)
    fw2t = np.ascontiguousarray(fw2.T)

    in_maps = []
    for c in range(8):
        b, h = c // 2, c % 2
        xb = x[b]
        if h == 1:
            xb = xb[::-1]
        xpad = np.zeros((IN_T, D), np.float32)
        xpad[PADL:PADL + CORE_T] = xb[0:CORE_T]
        xpad[PADL + CORE_T:] = xb[CORE_T:CORE_T + HALO]
        in_maps.append({
            "xin": np.ascontiguousarray(xpad),
            "W1": W1, "b1a": b1a, "b1g": b1g, "W2": W2, "b2b": b2b,
            "cswr": np.ascontiguousarray(csw_rep(h == 1)),
            "fw1t": fw1t, "fcw": cw_rep(h == 1), "fcb": cb_rs, "fw2t": fw2t,
            "identb": ident,
        })
    return in_maps


def kernel(L0_attn_inputs, L0_attn_W1, L0_attn_b1, L0_attn_W2, L0_attn_b2,
           L0_attn_conv_shared_weight, L0_ffn_w1, L0_ffn_conv_w, L0_ffn_conv_b,
           L0_ffn_w2):
    in_maps = prep_inputs(L0_attn_inputs, L0_attn_W1, L0_attn_b1, L0_attn_W2,
                          L0_attn_b2, L0_attn_conv_shared_weight, L0_ffn_w1,
                          L0_ffn_conv_w, L0_ffn_conv_b, L0_ffn_w2)
    nc = build_nc()
    res = run_bass_kernel_spmd(nc, in_maps, core_ids=list(range(8)))
    x = np.asarray(L0_attn_inputs)
    B, S, _ = x.shape
    out = np.empty((B, S, D), np.float32)
    for c in range(8):
        b, h = c // 2, c % 2
        oc = np.asarray(res.results[c]["out"])
        if h == 0:
            out[b, 0:CORE_T] = oc
        else:
            out[b, CORE_T:] = oc[::-1]
    return out
